# version 15
# baseline (speedup 1.0000x reference)
import numpy as np
from contextlib import ExitStack
import concourse.bass as bass
import concourse.mybir as mybir
from concourse.bass_utils import run_bass_kernel_spmd

F32 = mybir.dt.float32
BF16 = mybir.dt.bfloat16
AF = mybir.ActivationFunctionType
OP = mybir.AluOpType
AX = mybir.AxisListType

D = 1024
DI = 2048
NS = 16
RK = 64
NE = 8
EB = 512
NSL = DI // 128
NSD = D // 128
MYSL = 8
TL = 2560
R0 = 512
TR_ = 2048
TSEQ = 4096
TCS = 1024
NSC = TSEQ // TCS
EPS = 1e-5

CH_ALL = [(i, 512 * i, 512) for i in range(5)]
CH_REAL = [(i, R0 + 512 * i, 512) for i in range(4)]
CH_C = [(i, 512 * i, 512) for i in range(4)]

DBG = []


class Ev:
    __slots__ = ("sem", "val")
    def __init__(self, sem, val):
        self.sem, self.val = sem, val


class Buf:
    __slots__ = ("t", "w", "r")
    def __init__(self, t):
        self.t = t
        self.w = None
        self.r = []

    @property
    def ap(self):
        return self.t[:]


class Sched:
    def __init__(self, nc, stack, sem_names):
        self.nc = nc
        self.sems = {n: stack.enter_context(nc.semaphore(n)) for n in sem_names}
        self.counts = {n: 0 for n in sem_names}
        self.waited = {}
        self.qrr = {}

    def inc(self, inst, name, amt=1):
        inst.then_inc(self.sems[name], amt)
        self.counts[name] += amt
        return Ev(name, self.counts[name])

    def wait(self, eng, *evs):
        flat = []
        for ev in evs:
            if isinstance(ev, (tuple, list)):
                flat.extend(ev)
            else:
                flat.append(ev)
        for ev in flat:
            if ev is None:
                continue
            key = (id(eng), ev.sem)
            if self.waited.get(key, 0) >= ev.val:
                continue
            eng.wait_ge(self.sems[ev.sem], ev.val)
            self.waited[key] = ev.val

    def cur(self, name):
        return Ev(name, self.counts[name])

    def dma(self, eng, qname, out, in_, **kw):
        if "cond" in kw:
            sem = "dC0"
            self.wait(eng, self.cur(sem))
            return self.inc(eng.dma_start(out=out, in_=in_, **kw), sem, 16)
        i = self.qrr.setdefault(qname, 0) % 4
        self.qrr[qname] += 1
        sem = f"{qname}{i}"
        self.wait(eng, self.cur(sem))
        return self.inc(eng.dma_start(out=out, in_=in_, **kw), sem, 16)


class WrapEng:
    def __init__(self, s, eng, sem):
        self._s, self._e, self._n = s, eng, sem

    def __getattr__(self, name):
        fn = getattr(self._e, name)
        def call(*a, **k):
            self._s.wait(self._e, self._s.cur(self._n))
            return self._s.inc(fn(*a, **k), self._n)
        return call


def build_program(stop=None):
    nc = bass.Bass("TRN2", target_bir_lowering=False)
    st = ExitStack()
    s = Sched(nc, st, [f"dS{i}" for i in range(4)] + [f"dG{i}" for i in range(4)]
              + [f"dO{i}" for i in range(4)] + [f"dW{i}" for i in range(4)]
              + [f"dX{i}" for i in range(4)]
              + ["dC0", "V", "SC", "T", "G", "BAR", "CC"])
    V, SC, TE, G, SY = nc.vector, nc.scalar, nc.tensor, nc.gpsimd, nc.sync
    ENGINES = [TE, V, SC, G, SY]
    Vw = WrapEng(s, V, "V")
    Sw = WrapEng(s, SC, "SC")

    def din(name, shape, dt=F32):
        return nc.dram_tensor(name, shape, dt, kind="ExternalInput").ap()

    def dint(name, shape, dt):
        kind = "ExternalOutput" if name in DBG else "Internal"
        return nc.dram_tensor(name, shape, dt, kind=kind).ap()

    x_h = din("x", [D, TL])
    w_ip_h = din("w_ip", [D, 2 * DI])
    w_xp_h = din("w_xp", [128, NSL, 96])
    w_dt_h = din("w_dt", [RK, MYSL * 128])
    w_op_h = din("w_op", [DI, D])
    w_gl_h = din("w_gl", [D, DI])
    w_gate_h = din("w_gate", [128, 8, NE])
    w_wd_h = din("w_wd", [128, 8, EB])
    w_wu_h = din("w_wu", [128, 4, D])
    bd_h = din("bd", [128, 4])
    bu_h = din("bu", [NE, D])
    lng_h = din("ln_g", [128, 8])
    lnb_h = din("ln_b", [128, 8])
    cw_h = din("conv_w", [128, 4, NSL])
    cb_h = din("conv_b", [128, NSL])
    dtb_h = din("dt_b", [128, MYSL])
    alog_h = din("A_log", [128, MYSL * NS])
    dv_h = din("Dv", [128, MYSL])
    glb_h = din("glu_b", [128, 16])
    ident_h = din("ident", [128, 128])
    sel1_h = din("sel1", [8, 8 * 128])
    sel8_h = din("sel8", [128, 8 * 8])
    rep64_h = din("rep64", [NE, EB])
    hmask_h = din("hmask", [128, 1])
    out_h = nc.dram_tensor("out", [D, TR_], F32, kind="ExternalOutput").ap()

    r1_h = dint("r1_sp", [D, TL], F32)
    xmid_h = dint("xmid_sp", [D, TR_], F32)
    xdbl_src = dint("xdbl_src", [96, TR_], BF16)
    xdbl_g = dint("xdbl_g", [2, 96, TR_], BF16)
    xdblf_h = dint("xdblf", [96, TSEQ], BF16)
    xc_all = dint("xc_all", [DI, TR_], BF16)
    sz_all = dint("sz_all", [DI, TR_], BF16)
    xc_src = dint("xc_src", [MYSL * 128, TR_], BF16)
    xc_g = dint("xc_g", [2, MYSL * 128, TR_], BF16)
    xcs_h = dint("xcs_sp", [MYSL * 128, TSEQ], BF16)
    sz_src = dint("sz_src", [MYSL * 128, TR_], BF16)
    sz_g = dint("sz_g", [2, MYSL * 128, TR_], BF16)
    szs_h = dint("szs_sp", [MYSL * 128, TSEQ], BF16)
    yg_h = dint("yg_sp", [MYSL * 128, TSEQ], BF16)
    yg_srcs = [dint(f"yg_src{k}", [256, TSEQ], BF16) for k in range(4)]
    yg_gs = [dint(f"yg_g{k}", [2, 256, TSEQ], BF16) for k in range(4)]
    ygf_h = dint("ygf", [DI, TR_], BF16)

    def sbt(name, shape, dt, stk=st):
        return stk.enter_context(nc.sbuf_tensor(name, shape, dt))

    ps = [st.enter_context(nc.psum_tensor(f"ps{i}", [128, 512], F32)) for i in range(8)]
    psb = [Buf(p) for p in ps]
    rr = [0]

    def bank(pool=6):
        i = rr[0] % pool
        rr[0] += 1
        return psb[i]

    ident_f = sbt("ident_s", [128, 128], F32)
    sel1 = sbt("sel1_s", [8, 8, 128], F32)
    sel8 = sbt("sel8_s", [128, 8, 8], BF16)
    ones_b = sbt("ones_b", [128, 1], BF16)
    rep64 = sbt("rep64_s", [NE, EB], BF16)
    wgate = sbt("wgate", [128, 8, NE], BF16)
    wwd = sbt("wwd", [128, 8, EB], BF16)
    wwu = sbt("wwu", [128, 4, D], BF16)
    bu_s = sbt("bu_s", [NE, D], BF16)
    bd_s = sbt("bd_s", [128, 4], F32)
    lng = sbt("lng", [128, 8], F32)
    lnb = sbt("lnb", [128, 8], F32)
    cw = sbt("cw", [128, 4, NSL], F32)
    cb = sbt("cb", [128, NSL], F32)
    dtb = sbt("dtb", [128, MYSL], F32)
    dv = sbt("dv", [128, MYSL], F32)
    glb = sbt("glb", [128, 16], F32)
    a_sb = sbt("a_sb", [128, MYSL * NS], F32)
    wxp = sbt("wxp", [128, NSL, 96], BF16)
    wdt = sbt("wdt", [RK, MYSL * 128], BF16)
    hm = sbt("hm", [128, 1], F32)
    st_ln = sbt("st_ln", [8, 8, 512], F32)
    stat_ev = [None]

    blk = st.enter_context(nc.Block())

    def ld(dst, src, q="dS", **kw):
        eng = G if q in ("dG", "dW") else SY
        return s.dma(eng, q, dst, src, **kw)

    ld(ident_f[:], ident_h)
    ld(sel1[:], sel1_h.rearrange("a (c p) -> a c p", p=128))
    ld(sel8[:], sel8_h.rearrange("a (c e) -> a c e", e=8), q="dG")
    ld(rep64[:], rep64_h, q="dG")
    ld(wgate[:], w_gate_h, q="dG")
    ld(wwd[:], w_wd_h, q="dG")
    ld(wwu[:], w_wu_h, q="dG")
    ld(bu_s[:], bu_h, q="dG")
    ld(bd_s[:], bd_h)
    ld(lng[:], lng_h)
    ld(lnb[:], lnb_h)
    ld(cw[:], cw_h)
    ld(cb[:], cb_h)
    ld(dtb[:], dtb_h)
    ld(dv[:], dv_h)
    ld(glb[:], glb_h)
    ld(a_sb[:], alog_h)
    ld(hm[:], hmask_h)
    ld(wxp[:], w_xp_h, q="dG")
    ld(wdt[:], w_dt_h, q="dG")
    Vw.memset(ones_b[:], 1.0)
    for i in range(4):
        s.wait(SC, s.cur(f"dS{i}"))
    Sw.activation(out=a_sb[:], in_=a_sb[:], func=AF.Exp)
    s.wait(V, s.cur("SC"))
    Vw.tensor_scalar(a_sb[:], a_sb[:], -1.0, None, OP.mult)

    _ce = G.compute_val((G.partition_id() % 2) == 0, donate=False)
    _co = G.compute_val((G.partition_id() % 2) == 1, donate=False)

    def P_EVEN():
        return _ce

    def P_ODD():
        return _co

    def barrier():
        dma_evs = [s.cur(f"{q}{i}") for q in ("dS", "dG", "dO", "dW", "dX") for i in range(4)]
        dma_evs.append(s.cur("CC"))
        dma_evs.append(s.cur("dC0"))
        for eng in ENGINES:
            s.wait(eng, *dma_evs)
            eng.nop().then_inc(s.sems["BAR"], 1)
        s.counts["BAR"] += len(ENGINES)
        tgt = s.counts["BAR"]
        for eng in ENGINES:
            eng.wait_ge(s.sems["BAR"], tgt)
            s.waited[(id(eng), "BAR")] = tgt

    barrier()

    last_te = [None]

    def MM(*a, **k):
        last_te[0] = TE.matmul(*a, **k)
        return last_te[0]

    def TR(*a, **k):
        last_te[0] = TE.transpose(*a, **k)
        return last_te[0]

    def top():
        return s.inc(last_te[0], "T")

    def ln_ab(slot_a, slot_b):
        sx, sx2 = st_ln[:, 0, :], st_ln[:, 1, :]
        a_r, b_r = st_ln[:, slot_a, :], st_ln[:, slot_b, :]
        t6, t7 = st_ln[:, 6, :], st_ln[:, 7, :]
        s.wait(V, stat_ev[0])
        Vw.tensor_scalar(b_r, sx, 1.0 / D, None, OP.mult)
        Vw.tensor_tensor(t6, b_r, b_r, OP.mult)
        Vw.tensor_scalar(t7, sx2, 1.0 / D, None, OP.mult)
        Vw.tensor_tensor(t7, t7, t6, OP.subtract)
        Vw.tensor_scalar(t7, t7, EPS, None, OP.add)
        e = Vw.reciprocal(t6, t7)
        s.wait(SC, e)
        e2 = Sw.activation(out=a_r, in_=t6, func=AF.Sqrt)
        s.wait(V, e2)
        Vw.tensor_tensor(b_r, b_r, a_r, OP.mult)
        e3 = Vw.tensor_scalar(b_r, b_r, -1.0, None, OP.mult)
        stat_ev[0] = e3

    def make_rep(slot_a, slot_b, da, db, chunks):
        first = True
        for (row, off, w) in chunks:
            for slot, dst in ((slot_a, da), (slot_b, db)):
                bk = bank()
                s.wait(TE, *bk.r, bk.w, stat_ev[0])
                MM(bk.ap[0:128, 0:w], sel1[:, row, :], st_ln[:, slot, 0:w],
                   start=True, stop=True)
                bk.w = top()
                bk.r = []
                s.wait(V, bk.w, *(dst.r if first else []))
                e = Vw.tensor_copy(dst.t[:, off:off + w], bk.ap[0:128, 0:w])
                bk.r.append(e)
                if first:
                    dst.r = []
                dst.w = e
            first = False

    class StatAcc:
        def __init__(self, b1, b2, total):
            self.b1, self.b2 = b1, b2
            self.total = total
            self.n = 0
            self.row = 0
            self.w = 512

        def feed(self, xb_ap, x2_ap, wait_evs=()):
            first = self.n == 0
            last = self.n == self.total - 1
            if first:
                s.wait(V, *self.b1.r, self.b1.w, *self.b2.r, self.b2.w)
                Vw.memset(self.b1.ap[0:8, 0:512], 0.0)
                ez = Vw.memset(self.b2.ap[0:8, 0:512], 0.0)
                s.wait(TE, ez)
            s.wait(TE, *[e for e in wait_evs if e])
            MM(self.b1.ap[0:8, 0:self.w], sel8[:, self.row, :], xb_ap,
               start=first, stop=last)
            MM(self.b2.ap[0:8, 0:self.w], sel8[:, self.row, :], x2_ap,
               start=first, stop=last)
            ev = top()
            self.n += 1
            if last:
                self.b1.w = ev; self.b1.r = []
                self.b2.w = ev; self.b2.r = []
                s.wait(V, ev)
                Vw.tensor_copy(st_ln[:, 0, :], self.b1.ap[0:8, 0:512])
                e2 = Vw.tensor_copy(st_ln[:, 1, :], self.b2.ap[0:8, 0:512])
                self.b1.r.append(e2)
                self.b2.r.append(e2)
                stat_ev[0] = e2
            return ev

    stA = ExitStack()
    xf = Buf(sbt("xf", [128, TL], F32, stA))
    xbc_ = [Buf(sbt(f"xbc{i}", [128, 512], BF16, stA)) for i in range(2)]
    x2c_ = [Buf(sbt(f"x2c{i}", [128, 512], BF16, stA)) for i in range(2)]
    arep = Buf(sbt("arep", [128, TL], BF16, stA))
    brep = Buf(sbt("brep", [128, TL], BF16, stA))
    xn1 = [Buf(sbt(f"xn1_{i}", [128, TL], BF16, stA)) for i in range(NSD)]
    nrm = Buf(sbt("nrm", [128, TL], F32, stA))

    acc = StatAcc(psb[6], psb[7], NSD * len(CH_ALL))
    for sl in range(NSD):
        s.wait(SY, *xf.r)
        xf.w = ld(xf.ap, x_h.rearrange("(a p) t -> a p t", p=128)[sl], q="dX")
        xf.r = []
        for (row, off, w) in CH_ALL:
            xbc = xbc_[row % 2]
            x2c = x2c_[row % 2]
            acc.row, acc.w = row, w
            s.wait(SC, xf.w, *xbc.r, *x2c.r)
            e1 = Sw.activation(out=xbc.t[:, 0:w], in_=xf.t[:, off:off + w], func=AF.Copy)
            e2 = Sw.activation(out=x2c.t[:, 0:w], in_=xf.t[:, off:off + w], func=AF.Square)
            xbc.w, xbc.r = e1, []
            x2c.w, x2c.r = e2, []
            xf.r.append(e2)
            ev = acc.feed(xbc.t[:, 0:w], x2c.t[:, 0:w], (e1, e2))
            xbc.r.append(ev)
            x2c.r.append(ev)
    ln_ab(4, 5)
    make_rep(4, 5, arep, brep, CH_ALL)

    for sl in range(NSD):
        s.wait(SY, *xf.r)
        xf.w = ld(xf.ap, x_h.rearrange("(a p) t -> a p t", p=128)[sl], q="dX")
        xf.r = []
        s.wait(V, xf.w, arep.w, brep.w, *nrm.r)
        Vw.tensor_tensor(nrm.ap, xf.ap, arep.ap, OP.mult)
        e = Vw.tensor_tensor(xn1[sl].ap, nrm.ap, brep.ap, OP.add)
        nrm.w, nrm.r = e, []
        xf.r.append(e); arep.r.append(e); brep.r.append(e)
        xn1[sl].w = e

    def emit_ffn(xn_bufs, tag, evac_fn, stk, chunks, tok_total):
        ngrp = tok_total // 128
        lgf = Buf(sbt(f"lgf_{tag}", [8, tok_total], F32, stk))
        lg_tok = Buf(sbt(f"lgt_{tag}", [128, ngrp, NE], F32, stk))
        m1 = Buf(sbt(f"m1_{tag}", [128, ngrp], F32, stk))
        m2 = Buf(sbt(f"m2_{tag}", [128, ngrp], F32, stk))
        msk = Buf(sbt(f"msk_{tag}", [128, ngrp, NE], F32, stk))
        gt = Buf(sbt(f"gt_{tag}", [128, ngrp, NE], F32, stk))
        gfm = Buf(sbt(f"gfm_{tag}", [NE, tok_total], BF16, stk))
        hbuf = [Buf(sbt(f"h_{tag}{i}", [128, 512], BF16, stk)) for i in range(4)]
        first = True
        for (row, off, w) in chunks:
            bk = bank()
            s.wait(TE, *bk.r, bk.w, *[b.w for b in xn_bufs])
            for k in range(NSD):
                MM(bk.ap[0:NE, 0:w], wgate[:, k, :], xn_bufs[k].t[:, off:off + w],
                   start=(k == 0), stop=(k == NSD - 1))
            bk.w = top(); bk.r = []
            s.wait(V, bk.w, *(lgf.r if first else []))
            e = Vw.tensor_copy(lgf.t[:, off:off + w], bk.ap[0:NE, 0:w])
            bk.r.append(e)
            if first:
                lgf.r = []
            lgf.w = e
            first = False
        for (row, off, w) in chunks:
            bk = bank()
            s.wait(TE, *bk.r, bk.w, lgf.w)
            ng = w // 128
            for g in range(ng):
                TR(bk.ap[0:128, g * 8:(g + 1) * 8],
                   lgf.t[:, off + g * 128: off + (g + 1) * 128],
                   ident_f[0:8, 0:8])
            bk.w = top(); bk.r = []
            s.wait(V, bk.w)
            g0 = off // 128
            e = Vw.tensor_copy(
                lg_tok.t[:, g0:g0 + ng, :],
                bk.ap[0:128, 0:ng * 8].rearrange("p (a b) -> p a b", b=8))
            bk.r.append(e)
            lg_tok.w = e
        s.wait(V, lg_tok.w)
        Vw.tensor_reduce(m1.ap, lg_tok.ap, AX.X, OP.max)
        Vw.tensor_tensor(msk.ap, lg_tok.ap,
                         m1.ap.unsqueeze(2).broadcast_to([128, ngrp, NE]), OP.is_equal)
        Vw.scalar_tensor_tensor(msk.ap, msk.ap, -1e30, lg_tok.ap, OP.mult, OP.add)
        Vw.tensor_reduce(m2.ap, msk.ap, AX.X, OP.max)
        e = Vw.tensor_tensor(gt.ap, lg_tok.ap,
                             m1.ap.unsqueeze(2).broadcast_to([128, ngrp, NE]), OP.subtract)
        s.wait(SC, e)
        e2 = Sw.activation(out=gt.ap, in_=gt.ap, func=AF.Exp)
        s.wait(V, e2)
        Vw.tensor_tensor(msk.ap, lg_tok.ap,
                         m2.ap.unsqueeze(2).broadcast_to([128, ngrp, NE]), OP.is_ge)
        Vw.tensor_tensor(gt.ap, gt.ap, msk.ap, OP.mult)
        e3 = Vw.tensor_tensor(m2.ap, m2.ap, m1.ap, OP.subtract)
        s.wait(SC, e3)
        e4 = Sw.activation(out=m2.ap, in_=m2.ap, func=AF.Exp)
        s.wait(V, e4)
        Vw.tensor_scalar(m2.ap, m2.ap, 1.0, None, OP.add)
        Vw.reciprocal(m2.ap, m2.ap)
        gt.w = Vw.tensor_tensor(gt.ap, gt.ap,
                                m2.ap.unsqueeze(2).broadcast_to([128, ngrp, NE]), OP.mult)
        first = True
        for (row, off, w) in chunks:
            bk = bank()
            s.wait(TE, *bk.r, bk.w, gt.w)
            ng = w // 128
            g0 = off // 128
            for g in range(ng):
                TR(bk.ap[0:8, g * 128:(g + 1) * 128],
                   gt.t[:, g0 + g, :], ident_f[:])
            bk.w = top(); bk.r = []
            s.wait(V, bk.w, *(gfm.r if first else []))
            e = Vw.tensor_copy(gfm.t[:, off:off + w], bk.ap[0:8, 0:w])
            bk.r.append(e)
            if first:
                gfm.r = []
            gfm.w = e
            first = False
        for (row, off, w) in chunks:
            for oc in range(4):
                bk = bank()
                s.wait(TE, *bk.r, bk.w)
                for k in range(NSD):
                    MM(bk.ap[0:128, 0:w], wwd[:, k, oc * 128:(oc + 1) * 128],
                       xn_bufs[k].t[:, off:off + w],
                       start=(k == 0), stop=(k == NSD - 1))
                bk.w = top(); bk.r = []
                s.wait(SC, bk.w, *hbuf[oc].r)
                e = Sw.activation(out=hbuf[oc].t[:, 0:w], in_=bk.ap[0:128, 0:w],
                                  func=AF.Relu, bias=bd_s[:, oc:oc + 1])
                bk.r.append(e)
                hbuf[oc].w, hbuf[oc].r = e, []
            for oc in range(4):
                bk = bank()
                s.wait(TE, *bk.r, bk.w, gfm.w)
                MM(bk.ap[0:128, 0:w], rep64[:, oc * 128:(oc + 1) * 128],
                   gfm.t[:, off:off + w], start=True, stop=True)
                bk.w = top(); bk.r = []
                s.wait(V, bk.w, hbuf[oc].w)
                e = Vw.tensor_tensor(hbuf[oc].t[:, 0:w], hbuf[oc].t[:, 0:w],
                                     bk.ap[0:128, 0:w], OP.mult)
                bk.r.append(e)
                hbuf[oc].w = e
            for oc in range(NSD):
                bk = bank()
                s.wait(TE, *bk.r, bk.w, *[b.w for b in hbuf], gfm.w)
                for k in range(4):
                    MM(bk.ap[0:128, 0:w], wwu[:, k, oc * 128:(oc + 1) * 128],
                       hbuf[k].t[:, 0:w], start=(k == 0), stop=False)
                MM(bk.ap[0:128, 0:w], bu_s[:, oc * 128:(oc + 1) * 128],
                   gfm.t[:, off:off + w], start=False, stop=True)
                bk.w = top(); bk.r = []
                for hb in hbuf:
                    hb.r.append(bk.w)
                evac_fn(oc, row, off, w, bk)

    r1c = [Buf(sbt(f"r1c{i}", [128, 512], F32, stA)) for i in range(2)]
    r1b = [Buf(sbt(f"r1b{i}", [128, 512], BF16, stA)) for i in range(2)]
    r1q = [Buf(sbt(f"r1q{i}", [128, 512], BF16, stA)) for i in range(2)]
    xre = [Buf(sbt(f"xre{i}", [128, 512], F32, stA)) for i in range(2)]
    acc2 = StatAcc(psb[6], psb[7], len(CH_ALL) * NSD)
    _f1n = [0]
    _f1pend = [None]

    def _feed_pending(acc):
        p = _f1pend[0]
        if p is None:
            return
        row, rb, rq, c1, c2 = p
        acc.row, acc.w = row, 512
        ev = acc.feed(rb.ap, rq.ap, (c1, c2))
        rb.r.append(ev)
        rq.r.append(ev)
        _f1pend[0] = None

    def evac_ffn1(oc, row, off, w, bk):
        i = _f1n[0] % 2
        _f1n[0] += 1
        xr, rc, rb, rq = xre[i], r1c[i], r1b[i], r1q[i]
        s.wait(SY, *xr.r)
        xr.w = ld(xr.t[:, 0:w], x_h.rearrange("(a p) t -> a p t", p=128)[oc, :, off:off + w], q="dX")
        xr.r = []
        s.wait(V, bk.w, xr.w, *rc.r)
        e = Vw.scalar_tensor_tensor(rc.t[:, 0:w], bk.ap[0:128, 0:w], 0.25,
                                    xr.t[:, 0:w], OP.mult, OP.add)
        bk.r.append(e); xr.r.append(e)
        rc.w, rc.r = e, []
        s.wait(G, rc.w)
        e2 = s.dma(G, "dG", r1_h.rearrange("(a p) t -> a p t", p=128)[oc, :, off:off + w],
                   rc.t[:, 0:w])
        rc.r.append(e2)
        s.wait(SC, rc.w, *rb.r, *rq.r)
        c1 = Sw.activation(out=rb.t[:, 0:w], in_=rc.t[:, 0:w], func=AF.Copy)
        c2 = Sw.activation(out=rq.t[:, 0:w], in_=rc.t[:, 0:w], func=AF.Square)
        rb.w, rb.r = c1, []
        rq.w, rq.r = c2, []
        rc.r.append(c2)
        _feed_pending(acc2)
        _f1pend[0] = (row, rb, rq, c1, c2)

    emit_ffn(xn1, "f1", evac_ffn1, stA, CH_ALL, TL)
    _feed_pending(acc2)
    barrier()
    ln_ab(4, 5)
    t6, t7 = st_ln[:, 6, :], st_ln[:, 7, :]
    s.wait(V, stat_ev[0])
    Vw.tensor_tensor(t6, st_ln[:, 4, :], st_ln[:, 4, :], OP.mult)
    e = Vw.reciprocal(t7, t6)
    s.wait(V, e)
    Vw.tensor_scalar(t7, t7, -EPS, None, OP.add)
    Vw.tensor_tensor(t7, t7, t6, OP.mult)
    Vw.tensor_scalar(t7, t7, EPS, None, OP.add)
    e2 = Vw.reciprocal(t7, t7)
    s.wait(SC, e2)
    e3 = Sw.activation(out=t7, in_=t7, func=AF.Sqrt)
    s.wait(V, e3)
    Vw.tensor_tensor(st_ln[:, 2, :], st_ln[:, 4, :], t7, OP.mult)
    e4 = Vw.tensor_tensor(st_ln[:, 3, :], st_ln[:, 5, :], t7, OP.mult)
    stat_ev[0] = e4
    barrier()
    stA.close()
    if stop == "A":
        st.close()
        return nc

    stB = ExitStack()
    xf = Buf(sbt("xfB", [128, TL], F32, stB))
    nrmB = Buf(sbt("nrmB", [128, TL], F32, stB))
    arepB = Buf(sbt("arepB", [128, TL], BF16, stB))
    brepB = Buf(sbt("brepB", [128, TL], BF16, stB))
    xn2i = [Buf(sbt(f"xn2i{i}", [128, TL], BF16, stB)) for i in range(NSD)]
    wipc = [Buf(sbt(f"wipc{i}", [128, NSD, 256], BF16, stB)) for i in range(2)]
    xp_sl = Buf(sbt("xp_sl", [128, TL], BF16, stB))
    cbacc = Buf(sbt("cbacc", [128, TL], BF16, stB))
    xc_sl = [Buf(sbt(f"xc_sl{i}", [128, TL], BF16, stB)) for i in range(2)]
    zc = [Buf(sbt(f"zcB{i}", [128, 512], BF16, stB)) for i in range(2)]
    dtlr = Buf(sbt("dtlrB", [96, TR_], BF16, stB))

    make_rep(2, 3, arepB, brepB, CH_ALL)

    for sl in range(NSD):
        s.wait(SY, *xf.r)
        xf.w = ld(xf.ap, r1_h.rearrange("(a p) t -> a p t", p=128)[sl], q="dX")
        xf.r = []
        s.wait(V, xf.w, arepB.w, brepB.w, *nrmB.r)
        Vw.tensor_tensor(nrmB.ap, xf.ap, arepB.ap, OP.mult)
        e = Vw.tensor_tensor(nrmB.ap, nrmB.ap, brepB.ap, OP.add)
        nrmB.w, nrmB.r = e, []
        xf.r.append(e); arepB.r.append(e); brepB.r.append(e)
        s.wait(SC, e, *xn2i[sl].r)
        e2 = Sw.activation(out=xn2i[sl].ap, in_=nrmB.ap, func=AF.Identity,
                           scale=lng[:, sl:sl + 1], bias=lnb[:, sl:sl + 1])
        nrmB.r.append(e2)
        xn2i[sl].w, xn2i[sl].r = e2, []

    def rebind(old_buf, tensor):
        b = Buf(tensor)
        b.r = list(old_buf.r) + ([old_buf.w] if old_buf.w else [])
        return b

    xdbl = rebind(xf, xf.t)
    cacc = rebind(nrmB, nrmB.t)
    s.wait(V, *xdbl.r)
    ezero = Vw.memset(xdbl.t[0:96, R0:R0 + TR_], 0.0)
    xdbl.w, xdbl.r = ezero, []

    def _xproj(psl):
        xcb = xc_sl[psl % 2]
        for (row, off, w) in CH_REAL:
            bk = bank()
            s.wait(TE, *bk.r, bk.w, xcb.w)
            MM(bk.ap[0:96, 0:w], wxp[:, psl, :], xcb.t[:, off:off + w],
               start=True, stop=True)
            bk.w = top(); bk.r = []
            xcb.r.append(bk.w)
            s.wait(V, bk.w, xdbl.w)
            e = Vw.tensor_tensor(xdbl.t[0:96, off:off + w], xdbl.t[0:96, off:off + w],
                                 bk.ap[0:96, 0:w], OP.add)
            bk.r.append(e)
            xdbl.w = e

    wip_r = w_ip_h.rearrange("(k p) m -> p k m", p=128)
    for sl in range(NSL):
        wb = wipc[sl % 2]
        s.wait(G, *wb.r)
        wb.w = ld(wb.t[:, :, 0:128], wip_r[:, :, sl * 128:(sl + 1) * 128], q="dW")
        wb.r = []
        for (row, off, w) in CH_ALL:
            bk = bank()
            s.wait(TE, *bk.r, bk.w, wb.w, *[b.w for b in xn2i])
            for k in range(NSD):
                MM(bk.ap[0:128, 0:w], wb.t[:, k, 0:128],
                   xn2i[k].t[:, off:off + w], start=(k == 0), stop=(k == NSD - 1))
            bk.w = top(); bk.r = []
            wb.r.append(bk.w)
            s.wait(SC, bk.w, *(xp_sl.r if row == 0 else []))
            e = Sw.activation(out=xp_sl.t[:, off:off + w], in_=bk.ap[0:128, 0:w],
                              func=AF.Copy)
            bk.r.append(e)
            if row == 0:
                xp_sl.r = []
            xp_sl.w = e
        s.wait(V, xp_sl.w)
        ehm = Vw.tensor_scalar(xp_sl.t[:, R0 - 3:R0], xp_sl.t[:, R0 - 3:R0],
                               hm[:, 0:1], None, OP.mult)
        xp_sl.w = ehm
        s.wait(V, xp_sl.w, *cbacc.r, *cacc.r)
        cbacc.r = []
        cacc.r = []
        Vw.tensor_scalar(cbacc.ap, xp_sl.ap, cw[:, 3, sl:sl + 1], None, OP.mult)
        Vw.scalar_tensor_tensor(cbacc.t[:, 1:TL], xp_sl.t[:, 0:TL - 1], cw[:, 2, sl:sl + 1],
                                cbacc.t[:, 1:TL], OP.mult, OP.add)
        Vw.scalar_tensor_tensor(cbacc.t[:, 2:TL], xp_sl.t[:, 0:TL - 2], cw[:, 1, sl:sl + 1],
                                cbacc.t[:, 2:TL], OP.mult, OP.add)
        ec = Vw.scalar_tensor_tensor(cacc.t[:, 3:TL], xp_sl.t[:, 0:TL - 3], cw[:, 0, sl:sl + 1],
                                     cbacc.t[:, 3:TL], OP.mult, OP.add)
        cacc.w, cacc.r = ec, []
        cbacc.w = ec
        cbacc.r.append(ec)
        xp_sl.r.append(ec)
        xcb = xc_sl[sl % 2]
        s.wait(SC, ec, *xcb.r)
        exc = Sw.activation(out=xcb.t[:, 3:TL], in_=cacc.t[:, 3:TL], func=AF.Silu,
                            bias=cb[:, sl:sl + 1])
        cacc.r.append(exc)
        xcb.w, xcb.r = exc, []
        s.wait(G, xcb.w)
        e_loc = s.dma(G, "dG", xc_all.rearrange("(a p) t -> a p t", p=128)[sl],
                      xcb.t[:, R0:R0 + TR_])
        xcb.r.append(e_loc)
        if sl > 0:
            _xproj(sl - 1)
        if sl == NSL - 1:
            _xproj(sl)
    s.wait(SC, xdbl.w, *dtlr.r)
    ecast = Sw.activation(out=dtlr.ap, in_=xdbl.t[0:96, R0:R0 + TR_], func=AF.Copy)
    dtlr.w, dtlr.r = ecast, []
    xdbl.r.append(ecast)
    s.wait(G, dtlr.w)
    e_sp = s.dma(G, "dG", xdbl_src, dtlr.ap)
    dtlr.r.append(e_sp)

    s.wait(G, *[s.cur(f"dG{i}") for i in range(4)], s.cur("dC0"))
    _est = s.dma(G, "dG", xc_src, xc_all[MYSL * 128:DI, :])
    s.wait(G, _est)
    s.dma(G, "dG", xc_src, xc_all[0:MYSL * 128, :], cond=P_ODD())
    _exc1 = s.dma(G, "dG", xcs_h[:, 0:TR_], xc_all[0:MYSL * 128, :])
    _exc2 = s.dma(G, "dG", xcs_h[:, TR_:TSEQ], xc_all[MYSL * 128:DI, :])
    s.wait(G, *[s.cur(f"dG{i}") for i in range(4)], s.cur("dC0"))
    cc1 = s.inc(G.collective_compute("AllGather", OP.bypass,
                                     replica_groups=[[0, 1], [2, 3], [4, 5], [6, 7]],
                                     ins=[xdbl_src], outs=[xdbl_g]), "CC", 1)
    cc2 = s.inc(G.collective_compute("AllGather", OP.bypass,
                                     replica_groups=[[0, 1], [2, 3], [4, 5], [6, 7]],
                                     ins=[xc_src], outs=[xc_g]), "CC", 1)
    wipz = Buf(sbt("wipz", [128, NSD, DI], BF16, stB))
    wipz.w = ld(wipz.ap, wip_r[:, :, DI:2 * DI], q="dW")
    for sl in range(NSL):
        for ci, (row, off, w) in enumerate(CH_REAL):
            zcb = zc[ci % 2]
            bk = bank()
            s.wait(TE, *bk.r, bk.w, wipz.w)
            for k in range(NSD):
                MM(bk.ap[0:128, 0:w], wipz.t[:, k, sl * 128:(sl + 1) * 128],
                   xn2i[k].t[:, off:off + w], start=(k == 0), stop=(k == NSD - 1))
            bk.w = top(); bk.r = []
            s.wait(SC, bk.w, *zcb.r)
            e = Sw.activation(out=zcb.t[:, 0:w], in_=bk.ap[0:128, 0:w], func=AF.Silu)
            bk.r.append(e)
            zcb.w, zcb.r = e, []
            s.wait(G, zcb.w)
            lo = off - R0
            e_loc = s.dma(G, "dG", sz_all.rearrange("(a p) t -> a p t", p=128)[sl, :, lo:lo + w],
                          zcb.t[:, 0:w])
            zcb.r.append(e_loc)
    sz_spill_evs = [s.cur(f"dG{i}") for i in range(4)]
    s.wait(G, cc1, cc2)
    _exd1 = s.dma(G, "dG", xdblf_h[:, 0:TR_], dtlr.ap)
    _exd2 = s.dma(G, "dG", xdblf_h[:, TR_:TSEQ], dtlr.ap)
    s.wait(G, _exd1, _exd2)
    s.dma(G, "dG", xdblf_h[:, TR_:TSEQ], xdbl_g[1], cond=P_EVEN())
    e_xd = s.dma(G, "dG", xdblf_h[:, 0:TR_], xdbl_g[0], cond=P_ODD())
    dtlr.r.append(e_xd)
    dtlr.r.append(_exd1)
    dtlr.r.append(_exd2)
    s.wait(G, _exc1, _exc2)
    s.dma(G, "dG", xcs_h[:, TR_:TSEQ], xc_g[1], cond=P_EVEN())
    s.dma(G, "dG", xcs_h[:, 0:TR_], xc_g[0], cond=P_ODD())
    barrier()
    stB.close()
    if stop == "B1":
        st.close()
        return nc

    stS = ExitStack()
    dtlr2 = Buf(sbt("dtlr2", [RK, TCS], BF16, stS))
    ball = [Buf(sbt(f"ball{i}", [128, 8, TCS], BF16, stS)) for i in range(2)]
    call = [Buf(sbt(f"call{i}", [128, 8, TCS], BF16, stS)) for i in range(2)]
    dtt = [Buf(sbt(f"dtt{i}", [128, TCS], F32, stS)) for i in range(2)]
    sp1 = Buf(sbt("sp1", [128, TCS], F32, stS))
    dtu = [Buf(sbt(f"dtu{i}", [128, TCS], BF16, stS)) for i in range(2)]
    xcs = [Buf(sbt(f"xcs{i}", [128, TCS], BF16, stS)) for i in range(2)]
    dA = [Buf(sbt(f"dA{i}", [128, TCS], F32, stS)) for i in range(2)]
    qall = Buf(sbt("qall", [128, NS // 2, TCS], BF16, stS))
    hbig = Buf(sbt("hbig", [128, NS // 2, TCS], BF16, stS))
    yv = Buf(sbt("yv", [128, TCS], F32, stS))
    ygb = Buf(sbt("ygb", [128, TCS], BF16, stS))
    state = Buf(sbt("state", [128, MYSL * NS], F32, stS))

    state.w = Vw.memset(state.ap, 0.0)

    eo3 = []
    for tc in range(NSC):
        c0 = tc * TCS
        s.wait(SY, *dtlr2.r)
        dtlr2.w = ld(dtlr2.ap, xdblf_h[0:RK, c0:c0 + TCS], q="dX")
        dtlr2.r = []
        def _bcld(dst, r0_):
            return s.dma(G, "dG", dst,
                         xdblf_h[RK + r0_:RK + r0_ + 4, c0:c0 + TCS]
                         .unsqueeze(0).broadcast_to([128, 4, TCS]))
        s.wait(G, *ball[0].r)
        ball[0].w = (_bcld(ball[0].t[:, 0:4, :], 0),
                     _bcld(ball[0].t[:, 4:8, :], 4))
        ball[0].r = []
        s.wait(G, *ball[1].r, *call[0].r, *call[1].r)
        ball[1].w = (_bcld(ball[1].t[:, 0:4, :], 8),
                     _bcld(ball[1].t[:, 4:8, :], 12))
        call[0].w = (_bcld(call[0].t[:, 0:4, :], NS + 0),
                     _bcld(call[0].t[:, 4:8, :], NS + 4))
        call[1].w = (_bcld(call[1].t[:, 0:4, :], NS + 8),
                     _bcld(call[1].t[:, 4:8, :], NS + 12))
        ball[1].r = []
        call[0].r = []
        call[1].r = []
        def dt_chain(sl):
            dtb_ = dtt[sl % 2]
            dtu_ = dtu[sl % 2]
            xcb_ = xcs[sl % 2]
            s.wait(SY, *xcb_.r)
            xcb_.w = ld(xcb_.ap, xcs_h.rearrange("(a p) t -> a p t", p=128)[sl, :, c0:c0 + TCS], q="dX")
            xcb_.r = []
            for hh in range(TCS // 512):
                bk = bank()
                s.wait(TE, *bk.r, bk.w, dtlr2.w)
                MM(bk.ap[0:128, 0:512], wdt[:, sl * 128:(sl + 1) * 128],
                   dtlr2.t[:, hh * 512:(hh + 1) * 512],
                   start=True, stop=True)
                dtlr2.r.append(s.cur("T"))
                bk.w = top(); bk.r = []
                s.wait(SC, bk.w, *(dtb_.r if hh == 0 else []))
                e = Sw.activation(out=dtb_.t[:, hh * 512:(hh + 1) * 512],
                                  in_=bk.ap[0:128, 0:512], func=AF.Identity,
                                  bias=dtb[:, sl:sl + 1])
                bk.r.append(e)
                if hh == 0:
                    dtb_.r = []
                dtb_.w = e
            s.wait(V, dtb_.w, *sp1.r)
            e = Vw.tensor_scalar_min(sp1.ap, dtb_.ap, 20.0)
            sp1.w, sp1.r = e, []
            s.wait(SC, e)
            Sw.activation(out=sp1.ap, in_=sp1.ap, func=AF.Exp)
            e3 = Sw.activation(out=sp1.ap, in_=sp1.ap, func=AF.Ln, bias=1.0)
            sp1.w = e3
            s.wait(V, e3)
            e4 = Vw.tensor_tensor(dtb_.ap, dtb_.ap, sp1.ap, OP.max)
            dtb_.w = e4
            sp1.r.append(e4)
            s.wait(V, dtb_.w, xcb_.w, *dtu_.r)
            e5 = Vw.tensor_tensor(dtu_.ap, dtb_.ap, xcb_.ap, OP.mult)
            dtu_.w, dtu_.r = e5, []
            xcb_.r.append(e5)

        dt_chain(0)
        for sl in range(MYSL):
            if sl + 1 < MYSL:
                dt_chain(sl + 1)
            dtt_c = dtt[sl % 2]
            dtu_c = dtu[sl % 2]
            xcs_c = xcs[sl % 2]
            for half in range(2):
                n0 = half * 8
                s.wait(V, dtu_c.w, ball[half].w, *qall.r)
                eqa = Vw.tensor_tensor(qall.ap,
                                       dtu_c.ap.unsqueeze(1).broadcast_to([128, 8, TCS]),
                                       ball[half].ap, OP.mult)
                qall.w, qall.r = eqa, []
                dtu_c.r.append(eqa)
                ball[half].r.append(eqa)
                eh = None
                for n in range(8):
                    da = dA[n % 2]
                    s.wait(SC, dtt_c.w, *da.r)
                    ea = Sw.activation(out=da.ap, in_=dtt_c.ap, func=AF.Exp,
                                       scale=a_sb[:, sl * NS + n0 + n: sl * NS + n0 + n + 1])
                    da.w, da.r = ea, []
                    col = sl * NS + n0 + n
                    s.wait(V, da.w, qall.w)
                    eh = Vw.tensor_tensor_scan(hbig.t[:, n, :], da.ap, qall.t[:, n, :],
                                               state.t[:, col:col + 1], OP.mult, OP.add)
                    da.r.append(eh)
                qall.r.append(eh)
                dtt_c.r.append(eh)
                Vw.tensor_copy(state.t[:, sl * NS + n0: sl * NS + n0 + 8],
                               hbig.t[:, :, TCS - 1:TCS].rearrange("p a b -> p (a b)"))
                s.wait(V, call[half].w)
                hc = qall.t
                Vw.tensor_tensor(hc[:, 0:8, :], hbig.t[:, 0:8, :],
                                 call[half].ap, OP.mult)
                Vw.tensor_tensor(hc[:, 0:4, :], hc[:, 0:4, :], hc[:, 4:8, :], OP.add)
                Vw.tensor_tensor(hc[:, 0:2, :], hc[:, 0:2, :], hc[:, 2:4, :], OP.add)
                if half == 0:
                    e6 = Vw.tensor_tensor(yv.ap, hc[:, 0, :], hc[:, 1, :], OP.add)
                else:
                    Vw.tensor_tensor(sp1.ap, hc[:, 0, :], hc[:, 1, :], OP.add)
                    e6 = Vw.tensor_tensor(yv.ap, yv.ap, sp1.ap, OP.add)
                    sp1.r.append(e6)
                call[half].r.append(e6)
                qall.r.append(e6)
            for _ in range(2):
                bkw = bank()
                s.wait(TE, *bkw.r, bkw.w, s.cur("V"))
                MM(bkw.ap[0:64, 0:64], wdt[:, sl * 128:sl * 128 + 64],
                   dtlr2.t[:, 0:64], start=True, stop=True)
                bkw.w = top(); bkw.r = []
                dtlr2.r.append(bkw.w)
            e8 = Vw.scalar_tensor_tensor(yv.ap, xcs_c.ap, dv[:, sl:sl + 1], yv.ap,
                                         OP.mult, OP.add)
            xcs_c.r.append(e8)
            s.wait(SC, e8, *ygb.r)
            e9 = Sw.activation(out=ygb.ap, in_=yv.ap, func=AF.Copy)
            yv.r.append(e9)
            ygb.w, ygb.r = e9, []
            s.wait(G, ygb.w)
            eo = s.dma(G, "dG", yg_h.rearrange("(a p) t -> a p t", p=128)[sl, :, c0:c0 + TCS],
                       ygb.ap)
            ygb.r.append(eo)
            if tc == NSC - 1:
                eo3.append(eo)
                if sl % 2 == 1:
                    k = sl // 2
                    r0_, r1_ = (sl - 1) * 128, (sl + 1) * 128
                    s.wait(G, eo3[sl - 1], eo3[sl])
                    est = s.dma(G, "dG", yg_srcs[k], yg_h[r0_:r1_, :])
                    s.wait(G, est)
                    ccy = s.inc(G.collective_compute(
                        "AllGather", OP.bypass,
                        replica_groups=[[0, 1], [2, 3], [4, 5], [6, 7]],
                        ins=[yg_srcs[k]], outs=[yg_gs[k]]), "CC", 1)
                    _own = ygf_h[r0_:r1_, :]
                    _par = ygf_h[MYSL * 128 + r0_:MYSL * 128 + r1_, :]
                    s.wait(G, ccy)
                    _ea = s.dma(G, "dG", _own, yg_gs[k][0][:, 0:TR_])
                    _eb = s.dma(G, "dG", _par, yg_gs[k][1][:, 0:TR_])
                    s.wait(G, _ea, _eb)
                    s.dma(G, "dG", _own, yg_gs[k][0][:, TR_:TSEQ], cond=P_ODD())
                    s.dma(G, "dG", _par, yg_gs[k][1][:, TR_:TSEQ], cond=P_ODD())
    barrier()
    stS.close()
    if stop == "B2":
        st.close()
        return nc

    stC = ExitStack()
    mtl = [Buf(sbt(f"mtl{i}", [128, TR_], BF16, stC)) for i in range(NSD)]

    stC1 = ExitStack()
    wopc = [Buf(sbt(f"wopc{i}", [128, NSL, 128], BF16, stC1)) for i in range(2)]
    ygc = [Buf(sbt(f"ygc{i}", [128, NSL, 512], BF16, stC1)) for i in range(2)]
    szc1 = Buf(sbt("szc", [128, NSL, 512], BF16, stC1))
    wrr = w_op_h.rearrange("(k p) m -> p k m", p=128)
    _wn = [0]

    def _wopld(oc):
        b = wopc[_wn[0] % 2]
        _wn[0] += 1
        s.wait(G, *b.r)
        b.w = ld(b.ap, wrr[:, :, oc * 128:(oc + 1) * 128], q="dW")
        b.r = []
        return b

    def emit_c1_load(row, off, w):
        yb = ygc[row % 2]
        zb = szc1
        s.wait(SY, *yb.r, *zb.r)
        yb.w = ld(yb.ap, ygf_h.rearrange("(k p) t -> p k t", p=128)[:, :, off:off + w], q="dX")
        zb.w = ld(zb.ap, sz_all.rearrange("(k p) t -> p k t", p=128)[:, :, off:off + w], q="dX")
        yb.r = []
        zb.r = []
        s.wait(V, yb.w, zb.w)
        eg = Vw.tensor_tensor(yb.ap, yb.ap, zb.ap, OP.mult)
        yb.w = eg
        zb.r.append(eg)
        return yb

    def emit_c1_oc(yb, off, w, oc):
        wb = _wopld(oc)
        bk = bank()
        s.wait(TE, *bk.r, bk.w, wb.w, yb.w)
        for k in range(NSL):
            MM(bk.ap[0:128, 0:w], wb.t[:, k, :],
               yb.t[:, k, :], start=(k == 0), stop=(k == NSL - 1))
        bk.w = top(); bk.r = []
        yb.r.append(bk.w)
        wb.r.append(bk.w)
        s.wait(SC, bk.w)
        e = Sw.activation(out=mtl[oc].t[:, off:off + w],
                          in_=bk.ap[0:128, 0:w], func=AF.Gelu)
        bk.r.append(e)
        mtl[oc].w = e
        return e

    stC2 = ExitStack()
    a2rep = Buf(sbt("a2rep", [128, TR_], F32, stC2))
    b2rep = Buf(sbt("b2rep", [128, TR_], F32, stC2))
    make_rep(4, 5, a2rep, b2rep, CH_REAL_SH := [(1 + i, 512 * i, 512) for i in range(4)])
    wgl = Buf(sbt("wgl", [128, NSD, DI], BF16, stC2))
    sgc = [Buf(sbt(f"sgc{i}", [128, 512], BF16, stC2)) for i in range(2)]
    xmc = [Buf(sbt(f"xmc{i}", [128, 512], F32, stC2)) for i in range(2)]
    xmb = [Buf(sbt(f"xmb{i}", [128, 512], BF16, stC2)) for i in range(2)]
    xmq = [Buf(sbt(f"xmq{i}", [128, 512], BF16, stC2)) for i in range(2)]
    rre = [Buf(sbt(f"rre{i}", [128, 512], F32, stC2)) for i in range(2)]
    wgl.w = ld(wgl.ap, w_gl_h.rearrange("(k p) m -> p k m", p=128), q="dW")
    acc3 = StatAcc(psb[6], psb[7], len(CH_C) * NSD)
    _c2n = [0]
    _c2pend = [None]

    def _feed_pending3():
        p = _c2pend[0]
        if p is None:
            return
        row, rb, rq, c1, c2 = p
        acc3.row, acc3.w = row, 512
        ev = acc3.feed(rb.ap, rq.ap, (c1, c2))
        rb.r.append(ev)
        rq.r.append(ev)
        _c2pend[0] = None

    _r0, _o0, _w0 = CH_C[0]
    _ybp = emit_c1_load(_r0, _o0, _w0)
    c1ev = {0: [emit_c1_oc(_ybp, _o0, _w0, oc) for oc in range(NSD)]}
    _ybn = [None]
    for ci, (row, off, w) in enumerate(CH_C):
        for sl in range(NSD):
            if ci + 1 < len(CH_C):
                rn_, on_, wn_ = CH_C[ci + 1]
                if sl == 0:
                    _ybn[0] = emit_c1_load(rn_, on_, wn_)
                    c1ev[ci + 1] = []
                c1ev[ci + 1].append(emit_c1_oc(_ybn[0], on_, wn_, sl))
            i = _c2n[0] % 2
            _c2n[0] += 1
            sg, xm, xb_, xq_, rr_ = sgc[i], xmc[i], xmb[i], xmq[i], rre[i]
            s.wait(SY, *rr_.r)
            rr_.w = ld(rr_.ap, r1_h.rearrange("(a p) t -> a p t", p=128)[sl, :, R0 + off:R0 + off + w], q="dX")
            rr_.r = []
            bk1 = bank(); bk2 = bank()
            s.wait(TE, *bk1.r, bk1.w, *bk2.r, bk2.w, wgl.w, *c1ev[ci])
            for k in range(NSD):
                MM(bk1.ap[0:128, 0:w], wgl.t[:, k, sl * 128:(sl + 1) * 128],
                   mtl[k].t[:, off:off + w], start=(k == 0), stop=(k == NSD - 1))
            for k in range(NSD):
                MM(bk2.ap[0:128, 0:w], wgl.t[:, k, D + sl * 128: D + (sl + 1) * 128],
                   mtl[k].t[:, off:off + w], start=(k == 0), stop=(k == NSD - 1))
            evt = top()
            bk1.w = evt; bk1.r = []
            bk2.w = evt; bk2.r = []
            s.wait(SC, evt, *sg.r)
            e = Sw.activation(out=sg.ap, in_=bk2.ap[0:128, 0:512], func=AF.Sigmoid,
                              bias=glb[:, 8 + sl:9 + sl])
            bk2.r.append(e)
            sg.w, sg.r = e, []
            s.wait(V, e, bk1.w, rr_.w, a2rep.w, b2rep.w, *xm.r)
            Vw.scalar_tensor_tensor(xm.ap, bk1.ap[0:128, 0:512], glb[:, sl:sl + 1],
                                    sg.ap, OP.add, OP.mult)
            Vw.tensor_tensor(xm.ap, xm.ap, rr_.ap, OP.add)
            Vw.tensor_tensor(xm.ap, xm.ap, b2rep.t[:, off:off + w], OP.add)
            Vw.tensor_tensor(rr_.ap, rr_.ap, a2rep.t[:, off:off + w], OP.mult)
            e2 = Vw.tensor_tensor(xm.ap, xm.ap, rr_.ap, OP.add)
            rr_.r.append(e2); a2rep.r.append(e2); b2rep.r.append(e2)
            sg.r.append(e2); bk1.r.append(e2)
            xm.w, xm.r = e2, []
            s.wait(G, xm.w)
            ed = s.dma(G, "dG", xmid_h.rearrange("(a p) t -> a p t", p=128)[sl, :, off:off + w],
                       xm.ap)
            xm.r.append(ed)
            s.wait(SC, xm.w, *xb_.r, *xq_.r)
            c1 = Sw.activation(out=xb_.ap, in_=xm.ap, func=AF.Copy)
            c2 = Sw.activation(out=xq_.ap, in_=xm.ap, func=AF.Square)
            xb_.w, xb_.r = c1, []
            xq_.w, xq_.r = c2, []
            xm.r.append(c2)
            _feed_pending3()
            _c2pend[0] = (row, xb_, xq_, c1, c2)
    _feed_pending3()
    barrier()
    stC2.close()
    stC1.close()
    stC.close()
    if stop == "C2":
        st.close()
        return nc

    stC3 = ExitStack()
    ln_ab(2, 3)
    arepC = Buf(sbt("arepC", [128, TR_], BF16, stC3))
    brepC = Buf(sbt("brepC", [128, TR_], BF16, stC3))
    make_rep(2, 3, arepC, brepC, CH_C)
    xn3 = [Buf(sbt(f"xn3_{i}", [128, TR_], BF16, stC3)) for i in range(NSD)]
    xfC = Buf(sbt("xfC", [128, TR_], F32, stC3))
    nrmC = Buf(sbt("nrmC", [128, TR_], F32, stC3))
    for sl in range(NSD):
        s.wait(SY, *xfC.r)
        xfC.w = ld(xfC.ap, xmid_h.rearrange("(a p) t -> a p t", p=128)[sl], q="dX")
        xfC.r = []
        s.wait(V, xfC.w, arepC.w, brepC.w, *nrmC.r)
        Vw.tensor_tensor(nrmC.ap, xfC.ap, arepC.ap, OP.mult)
        e = Vw.tensor_tensor(xn3[sl].ap, nrmC.ap, brepC.ap, OP.add)
        nrmC.w, nrmC.r = e, []
        xfC.r.append(e); arepC.r.append(e); brepC.r.append(e)
        xn3[sl].w = e

    outc = Buf(sbt("outc", [128, 512], F32, stC3))
    xre2 = Buf(sbt("xre2", [128, 512], F32, stC3))

    def evac_ffn2(oc, row, off, w, bk):
        s.wait(SY, *xre2.r)
        xre2.w = ld(xre2.ap, xmid_h.rearrange("(a p) t -> a p t", p=128)[oc, :, off:off + w], q="dX")
        xre2.r = []
        s.wait(V, bk.w, xre2.w, *outc.r)
        e = Vw.scalar_tensor_tensor(outc.ap, bk.ap[0:128, 0:512], 0.25, xre2.ap,
                                    OP.mult, OP.add)
        bk.r.append(e); xre2.r.append(e)
        outc.w, outc.r = e, []
        s.wait(SY, outc.w)
        ed = s.dma(SY, "dO", out_h.rearrange("(a p) t -> a p t", p=128)[oc, :, off:off + w],
                   outc.ap)
        outc.r.append(ed)

    emit_ffn(xn3, "f2", evac_ffn2, stC3, CH_C, TR_)
    for q in ("dS", "dG", "dO", "dW", "dX"):
        for i in range(4):
            s.wait(SY, s.cur(f"{q}{i}"))
            s.wait(G, s.cur(f"{q}{i}"))
    s.wait(SY, s.cur("dC0"))
    s.wait(G, s.cur("dC0"))
    stC3.close()
    st.close()
    return nc


def kernel(**inp):
    x = np.asarray(inp["L0_attn_x"], dtype=np.float32)
    f32 = np.float32

    def c(a):
        return np.ascontiguousarray(np.asarray(a), dtype=f32)

    W = {
        "w_ip": c(np.asarray(inp["L0_attn_in_proj_w"]).T),
        "w_xp": c(np.asarray(inp["L0_attn_x_proj_w"]).T.reshape(NSL, 128, 96).transpose(1, 0, 2)),
        "w_op": c(np.asarray(inp["L0_attn_out_proj_w"]).T),
        "w_gl": c(np.asarray(inp["L0_attn_glu_w"]).T),
        "w_gate": c(np.asarray(inp["L0_ffn_w_gate"]).reshape(8, 128, NE).transpose(1, 0, 2)),
        "w_wd": c(np.asarray(inp["L0_ffn_Wd"]).transpose(1, 0, 2).reshape(D, EB)
                  .reshape(8, 128, EB).transpose(1, 0, 2)),
        "w_wu": c(np.asarray(inp["L0_ffn_Wu"]).reshape(EB, D)
                  .reshape(4, 128, D).transpose(1, 0, 2)),
        "bd": c(np.asarray(inp["L0_ffn_bd"]).reshape(EB).reshape(4, 128).T),
        "bu": c(inp["L0_ffn_bu"]),
        "ln_g": c(np.asarray(inp["L0_attn_ln_g"]).reshape(8, 128).T),
        "ln_b": c(np.asarray(inp["L0_attn_ln_b"]).reshape(8, 128).T),
        "conv_w": c(np.asarray(inp["L0_attn_conv_w"]).reshape(NSL, 128, 4).transpose(1, 2, 0)),
        "conv_b": c(np.asarray(inp["L0_attn_conv_b"]).reshape(NSL, 128).T),
        "glu_b": c(np.asarray(inp["L0_attn_glu_b"]).reshape(16, 128).T),
        "ident": np.eye(128, dtype=f32),
    }
    sel1 = np.zeros((8, 8, 128), dtype=f32)
    for cc_ in range(8):
        sel1[cc_, cc_, :] = 1.0
    W["sel1"] = np.ascontiguousarray(sel1.reshape(8, 8 * 128))
    sel8 = np.zeros((128, 8, 8), dtype=f32)
    for cc_ in range(8):
        sel8[:, cc_, cc_] = 1.0
    W["sel8"] = np.ascontiguousarray(sel8.reshape(128, 64))
    W["rep64"] = np.ascontiguousarray(np.repeat(np.eye(NE, dtype=f32), EB // NE, axis=1))

    w_dt_full = np.asarray(inp["L0_attn_dt_proj_w"]).T
    dtb_full = np.asarray(inp["L0_attn_dt_proj_b"]).reshape(NSL, 128)
    alog_full = np.asarray(inp["L0_attn_A_log"]).reshape(NSL, 128, NS)
    dv_full = np.asarray(inp["L0_attn_D"]).reshape(NSL, 128)

    nc = build_program()
    in_maps = []
    for core in range(8):
        b, h = core // 2, core % 2
        m = dict(W)
        xl = np.zeros((TL, D), dtype=f32)
        xl[R0:, :] = x[b, h * TR_:(h + 1) * TR_, :]
        if h == 1:
            xl[R0 - 3:R0, :] = x[b, h * TR_ - 3:h * TR_, :]
        m["x"] = c(xl.T)
        sls = slice(h * MYSL, (h + 1) * MYSL)
        m["w_dt"] = c(w_dt_full[:, h * MYSL * 128:(h + 1) * MYSL * 128])
        m["dt_b"] = c(dtb_full[sls].T)
        m["A_log"] = c(alog_full[sls].transpose(1, 0, 2).reshape(128, MYSL * NS))
        m["Dv"] = c(dv_full[sls].T)
        m["hmask"] = np.full((128, 1), float(h), dtype=f32)
        in_maps.append(m)
    res = run_bass_kernel_spmd(nc, in_maps, core_ids=list(range(8)))
    out = np.empty((4, 4096, 1024), dtype=np.float32)
    for core in range(8):
        b, h = core // 2, core % 2
        o = res.results[core]["out"]
        out[b, h * TR_:(h + 1) * TR_, :] = o.T
    return out


# version 16
# speedup vs baseline: 1.0475x; 1.0475x over previous
import numpy as np
from contextlib import ExitStack
import concourse.bass as bass
import concourse.mybir as mybir
from concourse.bass_utils import run_bass_kernel_spmd

F32 = mybir.dt.float32
BF16 = mybir.dt.bfloat16
AF = mybir.ActivationFunctionType
OP = mybir.AluOpType
AX = mybir.AxisListType

D = 1024
DI = 2048
NS = 16
RK = 64
NE = 8
EB = 512
NSL = DI // 128
NSD = D // 128
MYSL = 8
TL = 2560
R0 = 512
TR_ = 2048
TSEQ = 4096
TCS = 1024
NSC = TSEQ // TCS
EPS = 1e-5

CH_ALL = [(i, 512 * i, 512) for i in range(5)]
CH_REAL = [(i, R0 + 512 * i, 512) for i in range(4)]
CH_C = [(i, 512 * i, 512) for i in range(4)]

DBG = []


class Ev:
    __slots__ = ("sem", "val")
    def __init__(self, sem, val):
        self.sem, self.val = sem, val


class Buf:
    __slots__ = ("t", "w", "r")
    def __init__(self, t):
        self.t = t
        self.w = None
        self.r = []

    @property
    def ap(self):
        return self.t[:]


class Sched:
    def __init__(self, nc, stack, sem_names):
        self.nc = nc
        self.sems = {n: stack.enter_context(nc.semaphore(n)) for n in sem_names}
        self.counts = {n: 0 for n in sem_names}
        self.waited = {}
        self.qrr = {}

    def inc(self, inst, name, amt=1):
        inst.then_inc(self.sems[name], amt)
        self.counts[name] += amt
        return Ev(name, self.counts[name])

    def wait(self, eng, *evs):
        flat = []
        for ev in evs:
            if isinstance(ev, (tuple, list)):
                flat.extend(ev)
            else:
                flat.append(ev)
        for ev in flat:
            if ev is None:
                continue
            key = (id(eng), ev.sem)
            if self.waited.get(key, 0) >= ev.val:
                continue
            eng.wait_ge(self.sems[ev.sem], ev.val)
            self.waited[key] = ev.val

    def cur(self, name):
        return Ev(name, self.counts[name])

    def dma(self, eng, qname, out, in_, **kw):
        if "cond" in kw:
            sem = "dC0"
            self.wait(eng, self.cur(sem))
            return self.inc(eng.dma_start(out=out, in_=in_, **kw), sem, 16)
        i = self.qrr.setdefault(qname, 0) % 4
        self.qrr[qname] += 1
        sem = f"{qname}{i}"
        self.wait(eng, self.cur(sem))
        return self.inc(eng.dma_start(out=out, in_=in_, **kw), sem, 16)


class WrapEng:
    def __init__(self, s, eng, sem):
        self._s, self._e, self._n = s, eng, sem

    def __getattr__(self, name):
        fn = getattr(self._e, name)
        def call(*a, **k):
            self._s.wait(self._e, self._s.cur(self._n))
            return self._s.inc(fn(*a, **k), self._n)
        return call


def build_program(stop=None):
    nc = bass.Bass("TRN2", target_bir_lowering=False)
    st = ExitStack()
    s = Sched(nc, st, [f"dS{i}" for i in range(4)] + [f"dG{i}" for i in range(4)]
              + [f"dO{i}" for i in range(4)] + [f"dW{i}" for i in range(4)]
              + [f"dX{i}" for i in range(4)]
              + ["dC0", "V", "SC", "T", "G", "BAR", "CC"])
    V, SC, TE, G, SY = nc.vector, nc.scalar, nc.tensor, nc.gpsimd, nc.sync
    ENGINES = [TE, V, SC, G, SY]
    Vw = WrapEng(s, V, "V")
    Sw = WrapEng(s, SC, "SC")

    def din(name, shape, dt=F32):
        return nc.dram_tensor(name, shape, dt, kind="ExternalInput").ap()

    def dint(name, shape, dt):
        kind = "ExternalOutput" if name in DBG else "Internal"
        return nc.dram_tensor(name, shape, dt, kind=kind).ap()

    x_h = din("x", [D, TL])
    w_ip_h = din("w_ip", [D, 2 * DI])
    w_xp_h = din("w_xp", [128, NSL, 96])
    w_dt_h = din("w_dt", [RK, MYSL * 128])
    w_op_h = din("w_op", [DI, D])
    w_gl_h = din("w_gl", [D, DI])
    w_gate_h = din("w_gate", [128, 8, NE])
    w_wd_h = din("w_wd", [128, 8, EB])
    w_wu_h = din("w_wu", [128, 4, D])
    bd_h = din("bd", [128, 4])
    bu_h = din("bu", [NE, D])
    lng_h = din("ln_g", [128, 8])
    lnb_h = din("ln_b", [128, 8])
    cw_h = din("conv_w", [128, 4, NSL])
    cb_h = din("conv_b", [128, NSL])
    dtb_h = din("dt_b", [128, MYSL])
    alog_h = din("A_log", [128, MYSL * NS])
    dv_h = din("Dv", [128, MYSL])
    glb_h = din("glu_b", [128, 16])
    ident_h = din("ident", [128, 128])
    sel1_h = din("sel1", [8, 8 * 128])
    sel8_h = din("sel8", [128, 8 * 8])
    rep64_h = din("rep64", [NE, EB])
    hmask_h = din("hmask", [128, 1])
    out_h = nc.dram_tensor("out", [D, TR_], F32, kind="ExternalOutput").ap()

    r1_h = dint("r1_sp", [D, TL], F32)
    xmid_h = dint("xmid_sp", [D, TR_], F32)
    xdbl_src = dint("xdbl_src", [96, TR_], BF16)
    xdbl_g = dint("xdbl_g", [2, 96, TR_], BF16)
    xdblf_h = dint("xdblf", [96, TSEQ], BF16)
    xc_all = dint("xc_all", [DI, TR_], BF16)
    sz_all = dint("sz_all", [DI, TR_], BF16)
    xc_src = dint("xc_src", [MYSL * 128, TR_], BF16)
    xc_g = dint("xc_g", [2, MYSL * 128, TR_], BF16)
    xcs_h = dint("xcs_sp", [MYSL * 128, TSEQ], BF16)
    sz_src = dint("sz_src", [MYSL * 128, TR_], BF16)
    sz_g = dint("sz_g", [2, MYSL * 128, TR_], BF16)
    szs_h = dint("szs_sp", [MYSL * 128, TSEQ], BF16)
    yg_h = dint("yg_sp", [MYSL * 128, TSEQ], BF16)
    yg_srcs = [dint(f"yg_src{k}", [256, TSEQ], BF16) for k in range(4)]
    yg_gs = [dint(f"yg_g{k}", [2, 256, TSEQ], BF16) for k in range(4)]
    ygf_h = dint("ygf", [DI, TR_], BF16)

    def sbt(name, shape, dt, stk=st):
        return stk.enter_context(nc.sbuf_tensor(name, shape, dt))

    ps = [st.enter_context(nc.psum_tensor(f"ps{i}", [128, 512], F32)) for i in range(8)]
    psb = [Buf(p) for p in ps]
    rr = [0]

    def bank(pool=6):
        i = rr[0] % pool
        rr[0] += 1
        return psb[i]

    ident_f = sbt("ident_s", [128, 128], F32)
    sel1 = sbt("sel1_s", [8, 8, 128], F32)
    sel8 = sbt("sel8_s", [128, 8, 8], BF16)
    ones_b = sbt("ones_b", [128, 1], BF16)
    rep64 = sbt("rep64_s", [NE, EB], BF16)
    wgate = sbt("wgate", [128, 8, NE], BF16)
    wwd = sbt("wwd", [128, 8, EB], BF16)
    wwu = sbt("wwu", [128, 4, D], BF16)
    bu_s = sbt("bu_s", [NE, D], BF16)
    bd_s = sbt("bd_s", [128, 4], F32)
    lng = sbt("lng", [128, 8], F32)
    lnb = sbt("lnb", [128, 8], F32)
    cw = sbt("cw", [128, 4, NSL], F32)
    cb = sbt("cb", [128, NSL], F32)
    dtb = sbt("dtb", [128, MYSL], F32)
    dv = sbt("dv", [128, MYSL], F32)
    glb = sbt("glb", [128, 16], F32)
    a_sb = sbt("a_sb", [128, MYSL * NS], F32)
    wxp = sbt("wxp", [128, NSL, 96], BF16)
    wdt = sbt("wdt", [RK, MYSL * 128], BF16)
    hm = sbt("hm", [128, 1], F32)
    st_ln = sbt("st_ln", [8, 8, 512], F32)
    stat_ev = [None]

    blk = st.enter_context(nc.Block())

    def ld(dst, src, q="dS", **kw):
        eng = G if q in ("dG", "dW") else SY
        return s.dma(eng, q, dst, src, **kw)

    ld(ident_f[:], ident_h)
    ld(sel1[:], sel1_h.rearrange("a (c p) -> a c p", p=128))
    ld(sel8[:], sel8_h.rearrange("a (c e) -> a c e", e=8), q="dG")
    ld(rep64[:], rep64_h, q="dG")
    ld(wgate[:], w_gate_h, q="dG")
    ld(wwd[:], w_wd_h, q="dG")
    ld(wwu[:], w_wu_h, q="dG")
    ld(bu_s[:], bu_h, q="dG")
    ld(bd_s[:], bd_h)
    ld(lng[:], lng_h)
    ld(lnb[:], lnb_h)
    ld(cw[:], cw_h)
    ld(cb[:], cb_h)
    ld(dtb[:], dtb_h)
    ld(dv[:], dv_h)
    ld(glb[:], glb_h)
    ld(a_sb[:], alog_h)
    ld(hm[:], hmask_h)
    ld(wxp[:], w_xp_h, q="dG")
    ld(wdt[:], w_dt_h, q="dG")
    Vw.memset(ones_b[:], 1.0)
    for i in range(4):
        s.wait(SC, s.cur(f"dS{i}"))
    Sw.activation(out=a_sb[:], in_=a_sb[:], func=AF.Exp)
    s.wait(V, s.cur("SC"))
    Vw.tensor_scalar(a_sb[:], a_sb[:], -1.0, None, OP.mult)

    _ce = G.compute_val((G.partition_id() % 2) == 0, donate=False)
    _co = G.compute_val((G.partition_id() % 2) == 1, donate=False)

    def P_EVEN():
        return _ce

    def P_ODD():
        return _co

    def barrier():
        dma_evs = [s.cur(f"{q}{i}") for q in ("dS", "dG", "dO", "dW", "dX") for i in range(4)]
        dma_evs.append(s.cur("CC"))
        dma_evs.append(s.cur("dC0"))
        for eng in ENGINES:
            s.wait(eng, *dma_evs)
            eng.nop().then_inc(s.sems["BAR"], 1)
        s.counts["BAR"] += len(ENGINES)
        tgt = s.counts["BAR"]
        for eng in ENGINES:
            eng.wait_ge(s.sems["BAR"], tgt)
            s.waited[(id(eng), "BAR")] = tgt

    barrier()

    last_te = [None]

    def MM(*a, **k):
        last_te[0] = TE.matmul(*a, **k)
        return last_te[0]

    def TR(*a, **k):
        last_te[0] = TE.transpose(*a, **k)
        return last_te[0]

    def top():
        return s.inc(last_te[0], "T")

    def ln_ab(slot_a, slot_b):
        sx, sx2 = st_ln[:, 0, :], st_ln[:, 1, :]
        a_r, b_r = st_ln[:, slot_a, :], st_ln[:, slot_b, :]
        t6, t7 = st_ln[:, 6, :], st_ln[:, 7, :]
        s.wait(V, stat_ev[0])
        Vw.tensor_scalar(b_r, sx, 1.0 / D, None, OP.mult)
        Vw.tensor_tensor(t6, b_r, b_r, OP.mult)
        Vw.tensor_scalar(t7, sx2, 1.0 / D, None, OP.mult)
        Vw.tensor_tensor(t7, t7, t6, OP.subtract)
        Vw.tensor_scalar(t7, t7, EPS, None, OP.add)
        e = Vw.reciprocal(t6, t7)
        s.wait(SC, e)
        e2 = Sw.activation(out=a_r, in_=t6, func=AF.Sqrt)
        s.wait(V, e2)
        Vw.tensor_tensor(b_r, b_r, a_r, OP.mult)
        e3 = Vw.tensor_scalar(b_r, b_r, -1.0, None, OP.mult)
        stat_ev[0] = e3

    def make_rep(slot_a, slot_b, da, db, chunks):
        first = True
        for (row, off, w) in chunks:
            for slot, dst in ((slot_a, da), (slot_b, db)):
                bk = bank()
                s.wait(TE, *bk.r, bk.w, stat_ev[0])
                MM(bk.ap[0:128, 0:w], sel1[:, row, :], st_ln[:, slot, 0:w],
                   start=True, stop=True)
                bk.w = top()
                bk.r = []
                s.wait(V, bk.w, *(dst.r if first else []))
                e = Vw.tensor_copy(dst.t[:, off:off + w], bk.ap[0:128, 0:w])
                bk.r.append(e)
                if first:
                    dst.r = []
                dst.w = e
            first = False

    class StatAcc:
        def __init__(self, b1, b2, total):
            self.b1, self.b2 = b1, b2
            self.total = total
            self.n = 0
            self.row = 0
            self.w = 512

        def feed(self, xb_ap, x2_ap, wait_evs=()):
            first = self.n == 0
            last = self.n == self.total - 1
            if first:
                s.wait(V, *self.b1.r, self.b1.w, *self.b2.r, self.b2.w)
                Vw.memset(self.b1.ap[0:8, 0:512], 0.0)
                ez = Vw.memset(self.b2.ap[0:8, 0:512], 0.0)
                s.wait(TE, ez)
            s.wait(TE, *[e for e in wait_evs if e])
            MM(self.b1.ap[0:8, 0:self.w], sel8[:, self.row, :], xb_ap,
               start=first, stop=last)
            MM(self.b2.ap[0:8, 0:self.w], sel8[:, self.row, :], x2_ap,
               start=first, stop=last)
            ev = top()
            self.n += 1
            if last:
                self.b1.w = ev; self.b1.r = []
                self.b2.w = ev; self.b2.r = []
                s.wait(V, ev)
                Vw.tensor_copy(st_ln[:, 0, :], self.b1.ap[0:8, 0:512])
                e2 = Vw.tensor_copy(st_ln[:, 1, :], self.b2.ap[0:8, 0:512])
                self.b1.r.append(e2)
                self.b2.r.append(e2)
                stat_ev[0] = e2
            return ev

    stA = ExitStack()
    xf = Buf(sbt("xf", [128, TL], F32, stA))
    xbc_ = [Buf(sbt(f"xbc{i}", [128, 512], BF16, stA)) for i in range(2)]
    x2c_ = [Buf(sbt(f"x2c{i}", [128, 512], BF16, stA)) for i in range(2)]
    arep = Buf(sbt("arep", [128, TL], BF16, stA))
    brep = Buf(sbt("brep", [128, TL], BF16, stA))
    xn1 = [Buf(sbt(f"xn1_{i}", [128, TL], BF16, stA)) for i in range(NSD)]
    nrm = Buf(sbt("nrm", [128, TL], F32, stA))

    acc = StatAcc(psb[6], psb[7], NSD * len(CH_ALL))
    for sl in range(NSD):
        s.wait(SY, *xf.r)
        xf.w = ld(xf.ap, x_h.rearrange("(a p) t -> a p t", p=128)[sl], q="dX")
        xf.r = []
        for (row, off, w) in CH_ALL:
            xbc = xbc_[row % 2]
            x2c = x2c_[row % 2]
            acc.row, acc.w = row, w
            s.wait(SC, xf.w, *xbc.r, *x2c.r)
            e1 = Sw.activation(out=xbc.t[:, 0:w], in_=xf.t[:, off:off + w], func=AF.Copy)
            e2 = Sw.activation(out=x2c.t[:, 0:w], in_=xf.t[:, off:off + w], func=AF.Square)
            xbc.w, xbc.r = e1, []
            x2c.w, x2c.r = e2, []
            xf.r.append(e2)
            ev = acc.feed(xbc.t[:, 0:w], x2c.t[:, 0:w], (e1, e2))
            xbc.r.append(ev)
            x2c.r.append(ev)
    ln_ab(4, 5)
    make_rep(4, 5, arep, brep, CH_ALL)

    for sl in range(NSD):
        s.wait(SY, *xf.r)
        xf.w = ld(xf.ap, x_h.rearrange("(a p) t -> a p t", p=128)[sl], q="dX")
        xf.r = []
        s.wait(V, xf.w, arep.w, brep.w, *nrm.r)
        Vw.tensor_tensor(nrm.ap, xf.ap, arep.ap, OP.mult)
        e = Vw.tensor_tensor(xn1[sl].ap, nrm.ap, brep.ap, OP.add)
        nrm.w, nrm.r = e, []
        xf.r.append(e); arep.r.append(e); brep.r.append(e)
        xn1[sl].w = e

    def emit_ffn(xn_bufs, tag, evac_fn, stk, chunks, tok_total):
        ngrp = tok_total // 128
        lgf = Buf(sbt(f"lgf_{tag}", [8, tok_total], F32, stk))
        lg_tok = Buf(sbt(f"lgt_{tag}", [128, ngrp, NE], F32, stk))
        m1 = Buf(sbt(f"m1_{tag}", [128, ngrp], F32, stk))
        m2 = Buf(sbt(f"m2_{tag}", [128, ngrp], F32, stk))
        msk = Buf(sbt(f"msk_{tag}", [128, ngrp, NE], F32, stk))
        gt = Buf(sbt(f"gt_{tag}", [128, ngrp, NE], F32, stk))
        gfm = Buf(sbt(f"gfm_{tag}", [NE, tok_total], BF16, stk))
        hbuf = [Buf(sbt(f"h_{tag}{i}", [128, 512], BF16, stk)) for i in range(4)]
        first = True
        for (row, off, w) in chunks:
            bk = bank()
            s.wait(TE, *bk.r, bk.w, *[b.w for b in xn_bufs])
            for k in range(NSD):
                MM(bk.ap[0:NE, 0:w], wgate[:, k, :], xn_bufs[k].t[:, off:off + w],
                   start=(k == 0), stop=(k == NSD - 1))
            bk.w = top(); bk.r = []
            s.wait(V, bk.w, *(lgf.r if first else []))
            e = Vw.tensor_copy(lgf.t[:, off:off + w], bk.ap[0:NE, 0:w])
            bk.r.append(e)
            if first:
                lgf.r = []
            lgf.w = e
            first = False
        for (row, off, w) in chunks:
            bk = bank()
            s.wait(TE, *bk.r, bk.w, lgf.w)
            ng = w // 128
            for g in range(ng):
                TR(bk.ap[0:128, g * 8:(g + 1) * 8],
                   lgf.t[:, off + g * 128: off + (g + 1) * 128],
                   ident_f[0:8, 0:8])
            bk.w = top(); bk.r = []
            s.wait(V, bk.w)
            g0 = off // 128
            e = Vw.tensor_copy(
                lg_tok.t[:, g0:g0 + ng, :],
                bk.ap[0:128, 0:ng * 8].rearrange("p (a b) -> p a b", b=8))
            bk.r.append(e)
            lg_tok.w = e
        s.wait(V, lg_tok.w)
        Vw.tensor_reduce(m1.ap, lg_tok.ap, AX.X, OP.max)
        Vw.tensor_tensor(msk.ap, lg_tok.ap,
                         m1.ap.unsqueeze(2).broadcast_to([128, ngrp, NE]), OP.is_equal)
        Vw.scalar_tensor_tensor(msk.ap, msk.ap, -1e30, lg_tok.ap, OP.mult, OP.add)
        Vw.tensor_reduce(m2.ap, msk.ap, AX.X, OP.max)
        e = Vw.tensor_tensor(gt.ap, lg_tok.ap,
                             m1.ap.unsqueeze(2).broadcast_to([128, ngrp, NE]), OP.subtract)
        s.wait(SC, e)
        e2 = Sw.activation(out=gt.ap, in_=gt.ap, func=AF.Exp)
        s.wait(V, e2)
        Vw.tensor_tensor(msk.ap, lg_tok.ap,
                         m2.ap.unsqueeze(2).broadcast_to([128, ngrp, NE]), OP.is_ge)
        Vw.tensor_tensor(gt.ap, gt.ap, msk.ap, OP.mult)
        e3 = Vw.tensor_tensor(m2.ap, m2.ap, m1.ap, OP.subtract)
        s.wait(SC, e3)
        e4 = Sw.activation(out=m2.ap, in_=m2.ap, func=AF.Exp)
        s.wait(V, e4)
        Vw.tensor_scalar(m2.ap, m2.ap, 1.0, None, OP.add)
        Vw.reciprocal(m2.ap, m2.ap)
        gt.w = Vw.tensor_tensor(gt.ap, gt.ap,
                                m2.ap.unsqueeze(2).broadcast_to([128, ngrp, NE]), OP.mult)
        first = True
        for (row, off, w) in chunks:
            bk = bank()
            s.wait(TE, *bk.r, bk.w, gt.w)
            ng = w // 128
            g0 = off // 128
            for g in range(ng):
                TR(bk.ap[0:8, g * 128:(g + 1) * 128],
                   gt.t[:, g0 + g, :], ident_f[:])
            bk.w = top(); bk.r = []
            s.wait(V, bk.w, *(gfm.r if first else []))
            e = Vw.tensor_copy(gfm.t[:, off:off + w], bk.ap[0:8, 0:w])
            bk.r.append(e)
            if first:
                gfm.r = []
            gfm.w = e
            first = False
        for (row, off, w) in chunks:
            for oc in range(4):
                bk = bank()
                s.wait(TE, *bk.r, bk.w)
                for k in range(NSD):
                    MM(bk.ap[0:128, 0:w], wwd[:, k, oc * 128:(oc + 1) * 128],
                       xn_bufs[k].t[:, off:off + w],
                       start=(k == 0), stop=(k == NSD - 1))
                bk.w = top(); bk.r = []
                s.wait(SC, bk.w, *hbuf[oc].r)
                e = Sw.activation(out=hbuf[oc].t[:, 0:w], in_=bk.ap[0:128, 0:w],
                                  func=AF.Relu, bias=bd_s[:, oc:oc + 1])
                bk.r.append(e)
                hbuf[oc].w, hbuf[oc].r = e, []
            for oc in range(4):
                bk = bank()
                s.wait(TE, *bk.r, bk.w, gfm.w)
                MM(bk.ap[0:128, 0:w], rep64[:, oc * 128:(oc + 1) * 128],
                   gfm.t[:, off:off + w], start=True, stop=True)
                bk.w = top(); bk.r = []
                s.wait(V, bk.w, hbuf[oc].w)
                e = Vw.tensor_tensor(hbuf[oc].t[:, 0:w], hbuf[oc].t[:, 0:w],
                                     bk.ap[0:128, 0:w], OP.mult)
                bk.r.append(e)
                hbuf[oc].w = e
            for oc in range(NSD):
                bk = bank()
                s.wait(TE, *bk.r, bk.w, *[b.w for b in hbuf], gfm.w)
                for k in range(4):
                    MM(bk.ap[0:128, 0:w], wwu[:, k, oc * 128:(oc + 1) * 128],
                       hbuf[k].t[:, 0:w], start=(k == 0), stop=False)
                MM(bk.ap[0:128, 0:w], bu_s[:, oc * 128:(oc + 1) * 128],
                   gfm.t[:, off:off + w], start=False, stop=True)
                bk.w = top(); bk.r = []
                for hb in hbuf:
                    hb.r.append(bk.w)
                evac_fn(oc, row, off, w, bk)

    r1c = [Buf(sbt(f"r1c{i}", [128, 512], F32, stA)) for i in range(2)]
    r1b = [Buf(sbt(f"r1b{i}", [128, 512], BF16, stA)) for i in range(2)]
    r1q = [Buf(sbt(f"r1q{i}", [128, 512], BF16, stA)) for i in range(2)]
    xre = [Buf(sbt(f"xre{i}", [128, 512], F32, stA)) for i in range(2)]
    acc2 = StatAcc(psb[6], psb[7], len(CH_ALL) * NSD)
    _f1n = [0]
    _f1pend = [None]

    def _feed_pending(acc):
        p = _f1pend[0]
        if p is None:
            return
        row, rb, rq, c1, c2 = p
        acc.row, acc.w = row, 512
        ev = acc.feed(rb.ap, rq.ap, (c1, c2))
        rb.r.append(ev)
        rq.r.append(ev)
        _f1pend[0] = None

    def evac_ffn1(oc, row, off, w, bk):
        i = _f1n[0] % 2
        _f1n[0] += 1
        xr, rc, rb, rq = xre[i], r1c[i], r1b[i], r1q[i]
        s.wait(SY, *xr.r)
        xr.w = ld(xr.t[:, 0:w], x_h.rearrange("(a p) t -> a p t", p=128)[oc, :, off:off + w], q="dX")
        xr.r = []
        s.wait(V, bk.w, xr.w, *rc.r)
        e = Vw.scalar_tensor_tensor(rc.t[:, 0:w], bk.ap[0:128, 0:w], 0.25,
                                    xr.t[:, 0:w], OP.mult, OP.add)
        bk.r.append(e); xr.r.append(e)
        rc.w, rc.r = e, []
        s.wait(G, rc.w)
        e2 = s.dma(G, "dG", r1_h.rearrange("(a p) t -> a p t", p=128)[oc, :, off:off + w],
                   rc.t[:, 0:w])
        rc.r.append(e2)
        s.wait(SC, rc.w, *rb.r, *rq.r)
        c1 = Sw.activation(out=rb.t[:, 0:w], in_=rc.t[:, 0:w], func=AF.Copy)
        c2 = Sw.activation(out=rq.t[:, 0:w], in_=rc.t[:, 0:w], func=AF.Square)
        rb.w, rb.r = c1, []
        rq.w, rq.r = c2, []
        rc.r.append(c2)
        _feed_pending(acc2)
        _f1pend[0] = (row, rb, rq, c1, c2)

    emit_ffn(xn1, "f1", evac_ffn1, stA, CH_ALL, TL)
    _feed_pending(acc2)
    barrier()
    ln_ab(4, 5)
    t6, t7 = st_ln[:, 6, :], st_ln[:, 7, :]
    s.wait(V, stat_ev[0])
    Vw.tensor_tensor(t6, st_ln[:, 4, :], st_ln[:, 4, :], OP.mult)
    e = Vw.reciprocal(t7, t6)
    s.wait(V, e)
    Vw.tensor_scalar(t7, t7, -EPS, None, OP.add)
    Vw.tensor_tensor(t7, t7, t6, OP.mult)
    Vw.tensor_scalar(t7, t7, EPS, None, OP.add)
    e2 = Vw.reciprocal(t7, t7)
    s.wait(SC, e2)
    e3 = Sw.activation(out=t7, in_=t7, func=AF.Sqrt)
    s.wait(V, e3)
    Vw.tensor_tensor(st_ln[:, 2, :], st_ln[:, 4, :], t7, OP.mult)
    e4 = Vw.tensor_tensor(st_ln[:, 3, :], st_ln[:, 5, :], t7, OP.mult)
    stat_ev[0] = e4
    barrier()
    stA.close()
    if stop == "A":
        st.close()
        return nc

    stB = ExitStack()
    xf = Buf(sbt("xfB", [128, TL], F32, stB))
    nrmB = Buf(sbt("nrmB", [128, TL], F32, stB))
    arepB = Buf(sbt("arepB", [128, TL], BF16, stB))
    brepB = Buf(sbt("brepB", [128, TL], BF16, stB))
    xn2i = [Buf(sbt(f"xn2i{i}", [128, TL], BF16, stB)) for i in range(NSD)]
    wipc = [Buf(sbt(f"wipc{i}", [128, NSD, 256], BF16, stB)) for i in range(2)]
    xp_sl = Buf(sbt("xp_sl", [128, TL], BF16, stB))
    cbacc = Buf(sbt("cbacc", [128, TL], BF16, stB))
    xc_sl = [Buf(sbt(f"xc_sl{i}", [128, TL], BF16, stB)) for i in range(2)]
    zc = [Buf(sbt(f"zcB{i}", [128, 512], BF16, stB)) for i in range(2)]
    dtlr = Buf(sbt("dtlrB", [96, TR_], BF16, stB))

    make_rep(2, 3, arepB, brepB, CH_ALL)

    for sl in range(NSD):
        s.wait(SY, *xf.r)
        xf.w = ld(xf.ap, r1_h.rearrange("(a p) t -> a p t", p=128)[sl], q="dX")
        xf.r = []
        s.wait(V, xf.w, arepB.w, brepB.w, *nrmB.r)
        Vw.tensor_tensor(nrmB.ap, xf.ap, arepB.ap, OP.mult)
        e = Vw.tensor_tensor(nrmB.ap, nrmB.ap, brepB.ap, OP.add)
        nrmB.w, nrmB.r = e, []
        xf.r.append(e); arepB.r.append(e); brepB.r.append(e)
        s.wait(SC, e, *xn2i[sl].r)
        e2 = Sw.activation(out=xn2i[sl].ap, in_=nrmB.ap, func=AF.Identity,
                           scale=lng[:, sl:sl + 1], bias=lnb[:, sl:sl + 1])
        nrmB.r.append(e2)
        xn2i[sl].w, xn2i[sl].r = e2, []

    def rebind(old_buf, tensor):
        b = Buf(tensor)
        b.r = list(old_buf.r) + ([old_buf.w] if old_buf.w else [])
        return b

    xdbl = rebind(xf, xf.t)
    cacc = rebind(nrmB, nrmB.t)
    s.wait(V, *xdbl.r)
    ezero = Vw.memset(xdbl.t[0:96, R0:R0 + TR_], 0.0)
    xdbl.w, xdbl.r = ezero, []

    def _xproj(psl):
        xcb = xc_sl[psl % 2]
        for (row, off, w) in CH_REAL:
            bk = bank()
            s.wait(TE, *bk.r, bk.w, xcb.w)
            MM(bk.ap[0:96, 0:w], wxp[:, psl, :], xcb.t[:, off:off + w],
               start=True, stop=True)
            bk.w = top(); bk.r = []
            xcb.r.append(bk.w)
            s.wait(V, bk.w, xdbl.w)
            e = Vw.tensor_tensor(xdbl.t[0:96, off:off + w], xdbl.t[0:96, off:off + w],
                                 bk.ap[0:96, 0:w], OP.add)
            bk.r.append(e)
            xdbl.w = e

    wip_r = w_ip_h.rearrange("(k p) m -> p k m", p=128)
    for sl in range(NSL):
        wb = wipc[sl % 2]
        s.wait(G, *wb.r)
        wb.w = ld(wb.t[:, :, 0:128], wip_r[:, :, sl * 128:(sl + 1) * 128], q="dW")
        wb.r = []
        for (row, off, w) in CH_ALL:
            bk = bank()
            s.wait(TE, *bk.r, bk.w, wb.w, *[b.w for b in xn2i])
            for k in range(NSD):
                MM(bk.ap[0:128, 0:w], wb.t[:, k, 0:128],
                   xn2i[k].t[:, off:off + w], start=(k == 0), stop=(k == NSD - 1))
            bk.w = top(); bk.r = []
            wb.r.append(bk.w)
            s.wait(SC, bk.w, *(xp_sl.r if row == 0 else []))
            e = Sw.activation(out=xp_sl.t[:, off:off + w], in_=bk.ap[0:128, 0:w],
                              func=AF.Copy)
            bk.r.append(e)
            if row == 0:
                xp_sl.r = []
            xp_sl.w = e
        s.wait(V, xp_sl.w)
        ehm = Vw.tensor_scalar(xp_sl.t[:, R0 - 3:R0], xp_sl.t[:, R0 - 3:R0],
                               hm[:, 0:1], None, OP.mult)
        xp_sl.w = ehm
        s.wait(V, xp_sl.w, *cbacc.r, *cacc.r)
        cbacc.r = []
        cacc.r = []
        Vw.tensor_scalar(cbacc.ap, xp_sl.ap, cw[:, 3, sl:sl + 1], None, OP.mult)
        Vw.scalar_tensor_tensor(cbacc.t[:, 1:TL], xp_sl.t[:, 0:TL - 1], cw[:, 2, sl:sl + 1],
                                cbacc.t[:, 1:TL], OP.mult, OP.add)
        Vw.scalar_tensor_tensor(cbacc.t[:, 2:TL], xp_sl.t[:, 0:TL - 2], cw[:, 1, sl:sl + 1],
                                cbacc.t[:, 2:TL], OP.mult, OP.add)
        ec = Vw.scalar_tensor_tensor(cacc.t[:, 3:TL], xp_sl.t[:, 0:TL - 3], cw[:, 0, sl:sl + 1],
                                     cbacc.t[:, 3:TL], OP.mult, OP.add)
        cacc.w, cacc.r = ec, []
        cbacc.w = ec
        cbacc.r.append(ec)
        xp_sl.r.append(ec)
        xcb = xc_sl[sl % 2]
        s.wait(SC, ec, *xcb.r)
        exc = Sw.activation(out=xcb.t[:, 3:TL], in_=cacc.t[:, 3:TL], func=AF.Silu,
                            bias=cb[:, sl:sl + 1])
        cacc.r.append(exc)
        xcb.w, xcb.r = exc, []
        s.wait(G, xcb.w)
        e_loc = s.dma(G, "dG", xc_all.rearrange("(a p) t -> a p t", p=128)[sl],
                      xcb.t[:, R0:R0 + TR_])
        xcb.r.append(e_loc)
        if sl > 0:
            _xproj(sl - 1)
        if sl == NSL - 1:
            _xproj(sl)
    s.wait(SC, xdbl.w, *dtlr.r)
    ecast = Sw.activation(out=dtlr.ap, in_=xdbl.t[0:96, R0:R0 + TR_], func=AF.Copy)
    dtlr.w, dtlr.r = ecast, []
    xdbl.r.append(ecast)
    s.wait(G, dtlr.w)
    e_sp = s.dma(G, "dG", xdbl_src, dtlr.ap)
    dtlr.r.append(e_sp)

    s.wait(G, *[s.cur(f"dG{i}") for i in range(4)], s.cur("dC0"))
    _est = s.dma(G, "dG", xc_src, xc_all[MYSL * 128:DI, :])
    s.wait(G, _est)
    s.dma(G, "dG", xc_src, xc_all[0:MYSL * 128, :], cond=P_ODD())
    _exc1 = s.dma(G, "dG", xcs_h[:, 0:TR_], xc_all[0:MYSL * 128, :])
    _exc2 = s.dma(G, "dG", xcs_h[:, TR_:TSEQ], xc_all[MYSL * 128:DI, :])
    s.wait(G, *[s.cur(f"dG{i}") for i in range(4)], s.cur("dC0"))
    cc1 = s.inc(G.collective_compute("AllGather", OP.bypass,
                                     replica_groups=[[0, 1], [2, 3], [4, 5], [6, 7]],
                                     ins=[xdbl_src], outs=[xdbl_g]), "CC", 1)
    cc2 = s.inc(G.collective_compute("AllGather", OP.bypass,
                                     replica_groups=[[0, 1], [2, 3], [4, 5], [6, 7]],
                                     ins=[xc_src], outs=[xc_g]), "CC", 1)
    wipz = Buf(sbt("wipz", [128, NSD, DI], BF16, stB))
    wipz.w = ld(wipz.ap, wip_r[:, :, DI:2 * DI], q="dW")
    for sl in range(NSL):
        for ci, (row, off, w) in enumerate(CH_REAL):
            zcb = zc[ci % 2]
            bk = bank()
            s.wait(TE, *bk.r, bk.w, wipz.w)
            for k in range(NSD):
                MM(bk.ap[0:128, 0:w], wipz.t[:, k, sl * 128:(sl + 1) * 128],
                   xn2i[k].t[:, off:off + w], start=(k == 0), stop=(k == NSD - 1))
            bk.w = top(); bk.r = []
            s.wait(SC, bk.w, *zcb.r)
            e = Sw.activation(out=zcb.t[:, 0:w], in_=bk.ap[0:128, 0:w], func=AF.Silu)
            bk.r.append(e)
            zcb.w, zcb.r = e, []
            s.wait(G, zcb.w)
            lo = off - R0
            e_loc = s.dma(G, "dG", sz_all.rearrange("(a p) t -> a p t", p=128)[sl, :, lo:lo + w],
                          zcb.t[:, 0:w])
            zcb.r.append(e_loc)
    sz_spill_evs = [s.cur(f"dG{i}") for i in range(4)]
    s.wait(G, cc1, cc2)
    _exd1 = s.dma(G, "dG", xdblf_h[:, 0:TR_], dtlr.ap)
    _exd2 = s.dma(G, "dG", xdblf_h[:, TR_:TSEQ], dtlr.ap)
    s.wait(G, _exd1, _exd2)
    s.dma(G, "dG", xdblf_h[:, TR_:TSEQ], xdbl_g[1], cond=P_EVEN())
    e_xd = s.dma(G, "dG", xdblf_h[:, 0:TR_], xdbl_g[0], cond=P_ODD())
    dtlr.r.append(e_xd)
    dtlr.r.append(_exd1)
    dtlr.r.append(_exd2)
    s.wait(G, _exc1, _exc2)
    s.dma(G, "dG", xcs_h[:, TR_:TSEQ], xc_g[1], cond=P_EVEN())
    s.dma(G, "dG", xcs_h[:, 0:TR_], xc_g[0], cond=P_ODD())
    barrier()
    stB.close()
    if stop == "B1":
        st.close()
        return nc

    stS = ExitStack()
    dtlr2s = [Buf(sbt(f"dtlr2{i}", [RK, TCS], BF16, stS)) for i in range(2)]
    b0s = [Buf(sbt(f"ball0{i}", [128, 8, TCS], BF16, stS)) for i in range(2)]
    ball1 = Buf(sbt("ball1", [128, 8, TCS], BF16, stS))
    call = [Buf(sbt(f"call{i}", [128, 8, TCS], BF16, stS)) for i in range(2)]
    dtt = [Buf(sbt(f"dtt{i}", [128, TCS], F32, stS)) for i in range(2)]
    sp1 = Buf(sbt("sp1", [128, TCS], F32, stS))
    dtu = [Buf(sbt(f"dtu{i}", [128, TCS], BF16, stS)) for i in range(2)]
    xcs = [Buf(sbt(f"xcs{i}", [128, TCS], BF16, stS)) for i in range(2)]
    dA = [Buf(sbt(f"dA{i}", [128, TCS], F32, stS)) for i in range(2)]
    qall = Buf(sbt("qall", [128, NS // 2, TCS], BF16, stS))
    hbig = Buf(sbt("hbig", [128, NS // 2, TCS], BF16, stS))
    yv = Buf(sbt("yv", [128, TCS], F32, stS))
    ygb = Buf(sbt("ygb", [128, TCS], BF16, stS))
    state = Buf(sbt("state", [128, MYSL * NS], F32, stS))

    state.w = Vw.memset(state.ap, 0.0)

    def _bcld(dst, r0_, c0_):
        return s.dma(G, "dG", dst,
                     xdblf_h[RK + r0_:RK + r0_ + 4, c0_:c0_ + TCS]
                     .unsqueeze(0).broadcast_to([128, 4, TCS]))

    eo3 = []
    for tc in range(NSC):
        c0 = tc * TCS
        dtlr2 = dtlr2s[tc % 2]
        ball = [b0s[tc % 2], ball1]
        if tc == 0:
            s.wait(SY, *dtlr2.r)
            dtlr2.w = ld(dtlr2.ap, xdblf_h[0:RK, c0:c0 + TCS], q="dX")
            dtlr2.r = []
            s.wait(G, *ball[0].r)
            ball[0].w = (_bcld(ball[0].t[:, 0:4, :], 0, c0),
                         _bcld(ball[0].t[:, 4:8, :], 4, c0))
            ball[0].r = []
        s.wait(G, *ball[1].r, *call[0].r, *call[1].r)
        ball[1].w = (_bcld(ball[1].t[:, 0:4, :], 8, c0),
                     _bcld(ball[1].t[:, 4:8, :], 12, c0))
        call[0].w = (_bcld(call[0].t[:, 0:4, :], NS + 0, c0),
                     _bcld(call[0].t[:, 4:8, :], NS + 4, c0))
        call[1].w = (_bcld(call[1].t[:, 0:4, :], NS + 8, c0),
                     _bcld(call[1].t[:, 4:8, :], NS + 12, c0))
        ball[1].r = []
        call[0].r = []
        call[1].r = []
        def dt_chain(sl, c0_, dtl):
            dtb_ = dtt[sl % 2]
            dtu_ = dtu[sl % 2]
            xcb_ = xcs[sl % 2]
            s.wait(SY, *xcb_.r)
            xcb_.w = ld(xcb_.ap, xcs_h.rearrange("(a p) t -> a p t", p=128)[sl, :, c0_:c0_ + TCS], q="dX")
            xcb_.r = []
            for hh in range(TCS // 512):
                bk = bank()
                s.wait(TE, *bk.r, bk.w, dtl.w)
                MM(bk.ap[0:128, 0:512], wdt[:, sl * 128:(sl + 1) * 128],
                   dtl.t[:, hh * 512:(hh + 1) * 512],
                   start=True, stop=True)
                dtl.r.append(s.cur("T"))
                bk.w = top(); bk.r = []
                s.wait(SC, bk.w, *(dtb_.r if hh == 0 else []))
                e = Sw.activation(out=dtb_.t[:, hh * 512:(hh + 1) * 512],
                                  in_=bk.ap[0:128, 0:512], func=AF.Identity,
                                  bias=dtb[:, sl:sl + 1])
                bk.r.append(e)
                if hh == 0:
                    dtb_.r = []
                dtb_.w = e
            s.wait(V, dtb_.w, *sp1.r)
            e = Vw.tensor_scalar_min(sp1.ap, dtb_.ap, 20.0)
            sp1.w, sp1.r = e, []
            s.wait(SC, e)
            Sw.activation(out=sp1.ap, in_=sp1.ap, func=AF.Exp)
            e3 = Sw.activation(out=sp1.ap, in_=sp1.ap, func=AF.Ln, bias=1.0)
            sp1.w = e3
            s.wait(V, e3)
            e4 = Vw.tensor_tensor(dtb_.ap, dtb_.ap, sp1.ap, OP.max)
            dtb_.w = e4
            sp1.r.append(e4)
            s.wait(V, dtb_.w, xcb_.w, *dtu_.r)
            e5 = Vw.tensor_tensor(dtu_.ap, dtb_.ap, xcb_.ap, OP.mult)
            dtu_.w, dtu_.r = e5, []
            xcb_.r.append(e5)

        if tc == 0:
            dt_chain(0, c0, dtlr2)
        for sl in range(MYSL):
            if sl + 1 < MYSL:
                dt_chain(sl + 1, c0, dtlr2)
            elif tc + 1 < NSC:
                dt_chain(0, c0 + TCS, dtlr2s[(tc + 1) % 2])
            if sl == 5 and tc + 1 < NSC:
                nb = dtlr2s[(tc + 1) % 2]
                s.wait(SY, *nb.r)
                nb.w = ld(nb.ap, xdblf_h[0:RK, c0 + TCS:c0 + 2 * TCS], q="dX")
                nb.r = []
                n0b = b0s[(tc + 1) % 2]
                s.wait(G, *n0b.r)
                n0b.w = (_bcld(n0b.t[:, 0:4, :], 0, c0 + TCS),
                         _bcld(n0b.t[:, 4:8, :], 4, c0 + TCS))
                n0b.r = []
            dtt_c = dtt[sl % 2]
            dtu_c = dtu[sl % 2]
            xcs_c = xcs[sl % 2]
            for half in range(2):
                n0 = half * 8
                s.wait(V, dtu_c.w, ball[half].w, *qall.r)
                eqa = Vw.tensor_tensor(qall.ap,
                                       dtu_c.ap.unsqueeze(1).broadcast_to([128, 8, TCS]),
                                       ball[half].ap, OP.mult)
                qall.w, qall.r = eqa, []
                dtu_c.r.append(eqa)
                ball[half].r.append(eqa)
                eh = None
                for n in range(8):
                    da = dA[n % 2]
                    s.wait(SC, dtt_c.w, *da.r)
                    ea = Sw.activation(out=da.ap, in_=dtt_c.ap, func=AF.Exp,
                                       scale=a_sb[:, sl * NS + n0 + n: sl * NS + n0 + n + 1])
                    da.w, da.r = ea, []
                    col = sl * NS + n0 + n
                    s.wait(V, da.w, qall.w)
                    eh = Vw.tensor_tensor_scan(hbig.t[:, n, :], da.ap, qall.t[:, n, :],
                                               state.t[:, col:col + 1], OP.mult, OP.add)
                    da.r.append(eh)
                qall.r.append(eh)
                dtt_c.r.append(eh)
                Vw.tensor_copy(state.t[:, sl * NS + n0: sl * NS + n0 + 8],
                               hbig.t[:, :, TCS - 1:TCS].rearrange("p a b -> p (a b)"))
                s.wait(V, call[half].w)
                hc = qall.t
                Vw.tensor_tensor(hc[:, 0:8, :], hbig.t[:, 0:8, :],
                                 call[half].ap, OP.mult)
                Vw.tensor_tensor(hc[:, 0:4, :], hc[:, 0:4, :], hc[:, 4:8, :], OP.add)
                Vw.tensor_tensor(hc[:, 0:2, :], hc[:, 0:2, :], hc[:, 2:4, :], OP.add)
                if half == 0:
                    e6 = Vw.tensor_tensor(yv.ap, hc[:, 0, :], hc[:, 1, :], OP.add)
                else:
                    Vw.tensor_tensor(sp1.ap, hc[:, 0, :], hc[:, 1, :], OP.add)
                    e6 = Vw.tensor_tensor(yv.ap, yv.ap, sp1.ap, OP.add)
                    sp1.r.append(e6)
                call[half].r.append(e6)
                qall.r.append(e6)
            for _ in range(2):
                bkw = bank()
                s.wait(TE, *bkw.r, bkw.w, s.cur("V"))
                MM(bkw.ap[0:64, 0:64], wdt[:, sl * 128:sl * 128 + 64],
                   dtlr2.t[:, 0:64], start=True, stop=True)
                bkw.w = top(); bkw.r = []
                dtlr2.r.append(bkw.w)
            e8 = Vw.scalar_tensor_tensor(yv.ap, xcs_c.ap, dv[:, sl:sl + 1], yv.ap,
                                         OP.mult, OP.add)
            xcs_c.r.append(e8)
            s.wait(SC, e8, *ygb.r)
            e9 = Sw.activation(out=ygb.ap, in_=yv.ap, func=AF.Copy)
            yv.r.append(e9)
            ygb.w, ygb.r = e9, []
            s.wait(G, ygb.w)
            eo = s.dma(G, "dG", yg_h.rearrange("(a p) t -> a p t", p=128)[sl, :, c0:c0 + TCS],
                       ygb.ap)
            ygb.r.append(eo)
            if tc == NSC - 1:
                eo3.append(eo)
                if sl % 2 == 1:
                    k = sl // 2
                    r0_, r1_ = (sl - 1) * 128, (sl + 1) * 128
                    s.wait(G, eo3[sl - 1], eo3[sl])
                    est = s.dma(G, "dG", yg_srcs[k], yg_h[r0_:r1_, :])
                    s.wait(G, est)
                    ccy = s.inc(G.collective_compute(
                        "AllGather", OP.bypass,
                        replica_groups=[[0, 1], [2, 3], [4, 5], [6, 7]],
                        ins=[yg_srcs[k]], outs=[yg_gs[k]]), "CC", 1)
                    _own = ygf_h[r0_:r1_, :]
                    _par = ygf_h[MYSL * 128 + r0_:MYSL * 128 + r1_, :]
                    s.wait(G, ccy)
                    _ea = s.dma(G, "dG", _own, yg_gs[k][0][:, 0:TR_])
                    _eb = s.dma(G, "dG", _par, yg_gs[k][1][:, 0:TR_])
                    s.wait(G, _ea, _eb)
                    s.dma(G, "dG", _own, yg_gs[k][0][:, TR_:TSEQ], cond=P_ODD())
                    s.dma(G, "dG", _par, yg_gs[k][1][:, TR_:TSEQ], cond=P_ODD())
    barrier()
    stS.close()
    if stop == "B2":
        st.close()
        return nc

    stC = ExitStack()
    mtl = [Buf(sbt(f"mtl{i}", [128, TR_], BF16, stC)) for i in range(NSD)]

    stC1 = ExitStack()
    wopc = [Buf(sbt(f"wopc{i}", [128, NSL, 128], BF16, stC1)) for i in range(2)]
    ygc = [Buf(sbt(f"ygc{i}", [128, NSL, 512], BF16, stC1)) for i in range(2)]
    szc1 = Buf(sbt("szc", [128, NSL, 512], BF16, stC1))
    wrr = w_op_h.rearrange("(k p) m -> p k m", p=128)
    _wn = [0]

    def _wopld(oc):
        b = wopc[_wn[0] % 2]
        _wn[0] += 1
        s.wait(G, *b.r)
        b.w = ld(b.ap, wrr[:, :, oc * 128:(oc + 1) * 128], q="dW")
        b.r = []
        return b

    def emit_c1_chunk(row, off, w):
        yb = ygc[row % 2]
        zb = szc1
        s.wait(SY, *yb.r, *zb.r)
        yb.w = ld(yb.ap, ygf_h.rearrange("(k p) t -> p k t", p=128)[:, :, off:off + w], q="dX")
        zb.w = ld(zb.ap, sz_all.rearrange("(k p) t -> p k t", p=128)[:, :, off:off + w], q="dX")
        yb.r = []
        zb.r = []
        s.wait(V, yb.w, zb.w)
        eg = Vw.tensor_tensor(yb.ap, yb.ap, zb.ap, OP.mult)
        yb.w = eg
        zb.r.append(eg)
        wb = _wopld(0)
        for oc in range(NSD):
            wnext = _wopld(oc + 1) if oc + 1 < NSD else None
            bk = bank()
            s.wait(TE, *bk.r, bk.w, wb.w, yb.w)
            for k in range(NSL):
                MM(bk.ap[0:128, 0:w], wb.t[:, k, :],
                   yb.t[:, k, :], start=(k == 0), stop=(k == NSL - 1))
            bk.w = top(); bk.r = []
            yb.r.append(bk.w)
            wb.r.append(bk.w)
            s.wait(SC, bk.w)
            e = Sw.activation(out=mtl[oc].t[:, off:off + w],
                              in_=bk.ap[0:128, 0:w], func=AF.Gelu)
            bk.r.append(e)
            mtl[oc].w = e
            if wnext is not None:
                wb = wnext

    stC2 = ExitStack()
    a2rep = Buf(sbt("a2rep", [128, TR_], F32, stC2))
    b2rep = Buf(sbt("b2rep", [128, TR_], F32, stC2))
    make_rep(4, 5, a2rep, b2rep, CH_REAL_SH := [(1 + i, 512 * i, 512) for i in range(4)])
    wgl = Buf(sbt("wgl", [128, NSD, DI], BF16, stC2))
    sgc = [Buf(sbt(f"sgc{i}", [128, 512], BF16, stC2)) for i in range(2)]
    xmc = [Buf(sbt(f"xmc{i}", [128, 512], F32, stC2)) for i in range(2)]
    xmb = [Buf(sbt(f"xmb{i}", [128, 512], BF16, stC2)) for i in range(2)]
    xmq = [Buf(sbt(f"xmq{i}", [128, 512], BF16, stC2)) for i in range(2)]
    rre = [Buf(sbt(f"rre{i}", [128, 512], F32, stC2)) for i in range(2)]
    wgl.w = ld(wgl.ap, w_gl_h.rearrange("(k p) m -> p k m", p=128), q="dW")
    acc3 = StatAcc(psb[6], psb[7], len(CH_C) * NSD)
    _c2n = [0]
    _c2pend = [None]

    def _feed_pending3():
        p = _c2pend[0]
        if p is None:
            return
        row, rb, rq, c1, c2 = p
        acc3.row, acc3.w = row, 512
        ev = acc3.feed(rb.ap, rq.ap, (c1, c2))
        rb.r.append(ev)
        rq.r.append(ev)
        _c2pend[0] = None

    for (row, off, w) in CH_C:
        emit_c1_chunk(row, off, w)
        for sl in range(NSD):
            i = _c2n[0] % 2
            _c2n[0] += 1
            sg, xm, xb_, xq_, rr_ = sgc[i], xmc[i], xmb[i], xmq[i], rre[i]
            s.wait(SY, *rr_.r)
            rr_.w = ld(rr_.ap, r1_h.rearrange("(a p) t -> a p t", p=128)[sl, :, R0 + off:R0 + off + w], q="dX")
            rr_.r = []
            bk1 = bank(); bk2 = bank()
            s.wait(TE, *bk1.r, bk1.w, *bk2.r, bk2.w, wgl.w, *[b.w for b in mtl])
            for k in range(NSD):
                MM(bk1.ap[0:128, 0:w], wgl.t[:, k, sl * 128:(sl + 1) * 128],
                   mtl[k].t[:, off:off + w], start=(k == 0), stop=(k == NSD - 1))
            for k in range(NSD):
                MM(bk2.ap[0:128, 0:w], wgl.t[:, k, D + sl * 128: D + (sl + 1) * 128],
                   mtl[k].t[:, off:off + w], start=(k == 0), stop=(k == NSD - 1))
            evt = top()
            bk1.w = evt; bk1.r = []
            bk2.w = evt; bk2.r = []
            s.wait(SC, evt, *sg.r)
            e = Sw.activation(out=sg.ap, in_=bk2.ap[0:128, 0:512], func=AF.Sigmoid,
                              bias=glb[:, 8 + sl:9 + sl])
            bk2.r.append(e)
            sg.w, sg.r = e, []
            s.wait(V, e, bk1.w, rr_.w, a2rep.w, b2rep.w, *xm.r)
            Vw.scalar_tensor_tensor(xm.ap, bk1.ap[0:128, 0:512], glb[:, sl:sl + 1],
                                    sg.ap, OP.add, OP.mult)
            Vw.tensor_tensor(xm.ap, xm.ap, rr_.ap, OP.add)
            Vw.tensor_tensor(xm.ap, xm.ap, b2rep.t[:, off:off + w], OP.add)
            Vw.tensor_tensor(rr_.ap, rr_.ap, a2rep.t[:, off:off + w], OP.mult)
            e2 = Vw.tensor_tensor(xm.ap, xm.ap, rr_.ap, OP.add)
            rr_.r.append(e2); a2rep.r.append(e2); b2rep.r.append(e2)
            sg.r.append(e2); bk1.r.append(e2)
            xm.w, xm.r = e2, []
            s.wait(G, xm.w)
            ed = s.dma(G, "dG", xmid_h.rearrange("(a p) t -> a p t", p=128)[sl, :, off:off + w],
                       xm.ap)
            xm.r.append(ed)
            s.wait(SC, xm.w, *xb_.r, *xq_.r)
            c1 = Sw.activation(out=xb_.ap, in_=xm.ap, func=AF.Copy)
            c2 = Sw.activation(out=xq_.ap, in_=xm.ap, func=AF.Square)
            xb_.w, xb_.r = c1, []
            xq_.w, xq_.r = c2, []
            xm.r.append(c2)
            _feed_pending3()
            _c2pend[0] = (row, xb_, xq_, c1, c2)
    _feed_pending3()
    barrier()
    stC2.close()
    stC1.close()
    stC.close()
    if stop == "C2":
        st.close()
        return nc

    stC3 = ExitStack()
    ln_ab(2, 3)
    arepC = Buf(sbt("arepC", [128, TR_], BF16, stC3))
    brepC = Buf(sbt("brepC", [128, TR_], BF16, stC3))
    make_rep(2, 3, arepC, brepC, CH_C)
    xn3 = [Buf(sbt(f"xn3_{i}", [128, TR_], BF16, stC3)) for i in range(NSD)]
    xfC = Buf(sbt("xfC", [128, TR_], F32, stC3))
    nrmC = Buf(sbt("nrmC", [128, TR_], F32, stC3))
    for sl in range(NSD):
        s.wait(SY, *xfC.r)
        xfC.w = ld(xfC.ap, xmid_h.rearrange("(a p) t -> a p t", p=128)[sl], q="dX")
        xfC.r = []
        s.wait(V, xfC.w, arepC.w, brepC.w, *nrmC.r)
        Vw.tensor_tensor(nrmC.ap, xfC.ap, arepC.ap, OP.mult)
        e = Vw.tensor_tensor(xn3[sl].ap, nrmC.ap, brepC.ap, OP.add)
        nrmC.w, nrmC.r = e, []
        xfC.r.append(e); arepC.r.append(e); brepC.r.append(e)
        xn3[sl].w = e

    outc = Buf(sbt("outc", [128, 512], F32, stC3))
    xre2 = Buf(sbt("xre2", [128, 512], F32, stC3))

    def evac_ffn2(oc, row, off, w, bk):
        s.wait(SY, *xre2.r)
        xre2.w = ld(xre2.ap, xmid_h.rearrange("(a p) t -> a p t", p=128)[oc, :, off:off + w], q="dX")
        xre2.r = []
        s.wait(V, bk.w, xre2.w, *outc.r)
        e = Vw.scalar_tensor_tensor(outc.ap, bk.ap[0:128, 0:512], 0.25, xre2.ap,
                                    OP.mult, OP.add)
        bk.r.append(e); xre2.r.append(e)
        outc.w, outc.r = e, []
        s.wait(SY, outc.w)
        ed = s.dma(SY, "dO", out_h.rearrange("(a p) t -> a p t", p=128)[oc, :, off:off + w],
                   outc.ap)
        outc.r.append(ed)

    emit_ffn(xn3, "f2", evac_ffn2, stC3, CH_C, TR_)
    for q in ("dS", "dG", "dO", "dW", "dX"):
        for i in range(4):
            s.wait(SY, s.cur(f"{q}{i}"))
            s.wait(G, s.cur(f"{q}{i}"))
    s.wait(SY, s.cur("dC0"))
    s.wait(G, s.cur("dC0"))
    stC3.close()
    st.close()
    return nc


def kernel(**inp):
    x = np.asarray(inp["L0_attn_x"], dtype=np.float32)
    f32 = np.float32

    def c(a):
        return np.ascontiguousarray(np.asarray(a), dtype=f32)

    W = {
        "w_ip": c(np.asarray(inp["L0_attn_in_proj_w"]).T),
        "w_xp": c(np.asarray(inp["L0_attn_x_proj_w"]).T.reshape(NSL, 128, 96).transpose(1, 0, 2)),
        "w_op": c(np.asarray(inp["L0_attn_out_proj_w"]).T),
        "w_gl": c(np.asarray(inp["L0_attn_glu_w"]).T),
        "w_gate": c(np.asarray(inp["L0_ffn_w_gate"]).reshape(8, 128, NE).transpose(1, 0, 2)),
        "w_wd": c(np.asarray(inp["L0_ffn_Wd"]).transpose(1, 0, 2).reshape(D, EB)
                  .reshape(8, 128, EB).transpose(1, 0, 2)),
        "w_wu": c(np.asarray(inp["L0_ffn_Wu"]).reshape(EB, D)
                  .reshape(4, 128, D).transpose(1, 0, 2)),
        "bd": c(np.asarray(inp["L0_ffn_bd"]).reshape(EB).reshape(4, 128).T),
        "bu": c(inp["L0_ffn_bu"]),
        "ln_g": c(np.asarray(inp["L0_attn_ln_g"]).reshape(8, 128).T),
        "ln_b": c(np.asarray(inp["L0_attn_ln_b"]).reshape(8, 128).T),
        "conv_w": c(np.asarray(inp["L0_attn_conv_w"]).reshape(NSL, 128, 4).transpose(1, 2, 0)),
        "conv_b": c(np.asarray(inp["L0_attn_conv_b"]).reshape(NSL, 128).T),
        "glu_b": c(np.asarray(inp["L0_attn_glu_b"]).reshape(16, 128).T),
        "ident": np.eye(128, dtype=f32),
    }
    sel1 = np.zeros((8, 8, 128), dtype=f32)
    for cc_ in range(8):
        sel1[cc_, cc_, :] = 1.0
    W["sel1"] = np.ascontiguousarray(sel1.reshape(8, 8 * 128))
    sel8 = np.zeros((128, 8, 8), dtype=f32)
    for cc_ in range(8):
        sel8[:, cc_, cc_] = 1.0
    W["sel8"] = np.ascontiguousarray(sel8.reshape(128, 64))
    W["rep64"] = np.ascontiguousarray(np.repeat(np.eye(NE, dtype=f32), EB // NE, axis=1))

    w_dt_full = np.asarray(inp["L0_attn_dt_proj_w"]).T
    dtb_full = np.asarray(inp["L0_attn_dt_proj_b"]).reshape(NSL, 128)
    alog_full = np.asarray(inp["L0_attn_A_log"]).reshape(NSL, 128, NS)
    dv_full = np.asarray(inp["L0_attn_D"]).reshape(NSL, 128)

    nc = build_program()
    in_maps = []
    for core in range(8):
        b, h = core // 2, core % 2
        m = dict(W)
        xl = np.zeros((TL, D), dtype=f32)
        xl[R0:, :] = x[b, h * TR_:(h + 1) * TR_, :]
        if h == 1:
            xl[R0 - 3:R0, :] = x[b, h * TR_ - 3:h * TR_, :]
        m["x"] = c(xl.T)
        sls = slice(h * MYSL, (h + 1) * MYSL)
        m["w_dt"] = c(w_dt_full[:, h * MYSL * 128:(h + 1) * MYSL * 128])
        m["dt_b"] = c(dtb_full[sls].T)
        m["A_log"] = c(alog_full[sls].transpose(1, 0, 2).reshape(128, MYSL * NS))
        m["Dv"] = c(dv_full[sls].T)
        m["hmask"] = np.full((128, 1), float(h), dtype=f32)
        in_maps.append(m)
    res = run_bass_kernel_spmd(nc, in_maps, core_ids=list(range(8)))
    out = np.empty((4, 4096, 1024), dtype=np.float32)
    for core in range(8):
        b, h = core // 2, core % 2
        o = res.results[core]["out"]
        out[b, h * TR_:(h + 1) * TR_, :] = o.T
    return out


# version 17 (final)
# speedup vs baseline: 1.0488x; 1.0012x over previous
import numpy as np
from contextlib import ExitStack
import concourse.bass as bass
import concourse.mybir as mybir
from concourse.bass_utils import run_bass_kernel_spmd

F32 = mybir.dt.float32
BF16 = mybir.dt.bfloat16
AF = mybir.ActivationFunctionType
OP = mybir.AluOpType
AX = mybir.AxisListType

D = 1024
DI = 2048
NS = 16
RK = 64
NE = 8
EB = 512
NSL = DI // 128
NSD = D // 128
MYSL = 8
TL = 2560
R0 = 512
TR_ = 2048
TSEQ = 4096
TCS = 1024
NSC = TSEQ // TCS
EPS = 1e-5

CH_ALL = [(i, 512 * i, 512) for i in range(5)]
CH_REAL = [(i, R0 + 512 * i, 512) for i in range(4)]
CH_C = [(i, 512 * i, 512) for i in range(4)]

DBG = []


class Ev:
    __slots__ = ("sem", "val")
    def __init__(self, sem, val):
        self.sem, self.val = sem, val


class Buf:
    __slots__ = ("t", "w", "r")
    def __init__(self, t):
        self.t = t
        self.w = None
        self.r = []

    @property
    def ap(self):
        return self.t[:]


class Sched:
    def __init__(self, nc, stack, sem_names):
        self.nc = nc
        self.sems = {n: stack.enter_context(nc.semaphore(n)) for n in sem_names}
        self.counts = {n: 0 for n in sem_names}
        self.waited = {}
        self.qrr = {}

    def inc(self, inst, name, amt=1):
        inst.then_inc(self.sems[name], amt)
        self.counts[name] += amt
        return Ev(name, self.counts[name])

    def wait(self, eng, *evs):
        flat = []
        for ev in evs:
            if isinstance(ev, (tuple, list)):
                flat.extend(ev)
            else:
                flat.append(ev)
        for ev in flat:
            if ev is None:
                continue
            key = (id(eng), ev.sem)
            if self.waited.get(key, 0) >= ev.val:
                continue
            eng.wait_ge(self.sems[ev.sem], ev.val)
            self.waited[key] = ev.val

    def cur(self, name):
        return Ev(name, self.counts[name])

    def dma(self, eng, qname, out, in_, **kw):
        if "cond" in kw:
            sem = "dC0"
            self.wait(eng, self.cur(sem))
            return self.inc(eng.dma_start(out=out, in_=in_, **kw), sem, 16)
        i = self.qrr.setdefault(qname, 0) % 4
        self.qrr[qname] += 1
        sem = f"{qname}{i}"
        self.wait(eng, self.cur(sem))
        return self.inc(eng.dma_start(out=out, in_=in_, **kw), sem, 16)


class WrapEng:
    def __init__(self, s, eng, sem):
        self._s, self._e, self._n = s, eng, sem

    def __getattr__(self, name):
        fn = getattr(self._e, name)
        def call(*a, **k):
            self._s.wait(self._e, self._s.cur(self._n))
            return self._s.inc(fn(*a, **k), self._n)
        return call


def build_program(stop=None):
    nc = bass.Bass("TRN2", target_bir_lowering=False)
    st = ExitStack()
    s = Sched(nc, st, [f"dS{i}" for i in range(4)] + [f"dG{i}" for i in range(4)]
              + [f"dO{i}" for i in range(4)] + [f"dW{i}" for i in range(4)]
              + [f"dX{i}" for i in range(4)]
              + ["dC0", "V", "SC", "T", "G", "BAR", "CC"])
    V, SC, TE, G, SY = nc.vector, nc.scalar, nc.tensor, nc.gpsimd, nc.sync
    ENGINES = [TE, V, SC, G, SY]
    Vw = WrapEng(s, V, "V")
    Sw = WrapEng(s, SC, "SC")

    def din(name, shape, dt=F32):
        return nc.dram_tensor(name, shape, dt, kind="ExternalInput").ap()

    def dint(name, shape, dt):
        kind = "ExternalOutput" if name in DBG else "Internal"
        return nc.dram_tensor(name, shape, dt, kind=kind).ap()

    x_h = din("x", [D, TL])
    w_ip_h = din("w_ip", [D, 2 * DI])
    w_xp_h = din("w_xp", [128, NSL, 96])
    w_dt_h = din("w_dt", [RK, MYSL * 128])
    w_op_h = din("w_op", [DI, D])
    w_gl_h = din("w_gl", [D, DI])
    w_gate_h = din("w_gate", [128, 8, NE])
    w_wd_h = din("w_wd", [128, 8, EB])
    w_wu_h = din("w_wu", [128, 4, D])
    bd_h = din("bd", [128, 4])
    bu_h = din("bu", [NE, D])
    lng_h = din("ln_g", [128, 8])
    lnb_h = din("ln_b", [128, 8])
    cw_h = din("conv_w", [128, 4, NSL])
    cb_h = din("conv_b", [128, NSL])
    dtb_h = din("dt_b", [128, MYSL])
    alog_h = din("A_log", [128, MYSL * NS])
    dv_h = din("Dv", [128, MYSL])
    glb_h = din("glu_b", [128, 16])
    ident_h = din("ident", [128, 128])
    sel1_h = din("sel1", [8, 8 * 128])
    sel8_h = din("sel8", [128, 8 * 8])
    rep64_h = din("rep64", [NE, EB])
    hmask_h = din("hmask", [128, 1])
    out_h = nc.dram_tensor("out", [D, TR_], F32, kind="ExternalOutput").ap()

    r1_h = dint("r1_sp", [D, TL], F32)
    xmid_h = dint("xmid_sp", [D, TR_], F32)
    xdbl_src = dint("xdbl_src", [96, TR_], BF16)
    xdbl_g = dint("xdbl_g", [2, 96, TR_], BF16)
    xdblf_h = dint("xdblf", [96, TSEQ], BF16)
    xc_all = dint("xc_all", [DI, TR_], BF16)
    sz_all = dint("sz_all", [DI, TR_], BF16)
    xc_src = dint("xc_src", [MYSL * 128, TR_], BF16)
    xc_g = dint("xc_g", [2, MYSL * 128, TR_], BF16)
    xcs_h = dint("xcs_sp", [MYSL * 128, TSEQ], BF16)
    sz_src = dint("sz_src", [MYSL * 128, TR_], BF16)
    sz_g = dint("sz_g", [2, MYSL * 128, TR_], BF16)
    szs_h = dint("szs_sp", [MYSL * 128, TSEQ], BF16)
    yg_h = dint("yg_sp", [MYSL * 128, TSEQ], BF16)
    yg_srcs = [dint(f"yg_src{k}", [256, TSEQ], BF16) for k in range(4)]
    yg_gs = [dint(f"yg_g{k}", [2, 256, TSEQ], BF16) for k in range(4)]
    ygf_h = dint("ygf", [DI, TR_], BF16)

    def sbt(name, shape, dt, stk=st):
        return stk.enter_context(nc.sbuf_tensor(name, shape, dt))

    ps = [st.enter_context(nc.psum_tensor(f"ps{i}", [128, 512], F32)) for i in range(8)]
    psb = [Buf(p) for p in ps]
    rr = [0]

    def bank(pool=6):
        i = rr[0] % pool
        rr[0] += 1
        return psb[i]

    ident_f = sbt("ident_s", [128, 128], F32)
    sel1 = sbt("sel1_s", [8, 8, 128], F32)
    sel8 = sbt("sel8_s", [128, 8, 8], BF16)
    ones_b = sbt("ones_b", [128, 1], BF16)
    rep64 = sbt("rep64_s", [NE, EB], BF16)
    wgate = sbt("wgate", [128, 8, NE], BF16)
    wwd = sbt("wwd", [128, 8, EB], BF16)
    wwu = sbt("wwu", [128, 4, D], BF16)
    bu_s = sbt("bu_s", [NE, D], BF16)
    bd_s = sbt("bd_s", [128, 4], F32)
    lng = sbt("lng", [128, 8], F32)
    lnb = sbt("lnb", [128, 8], F32)
    cw = sbt("cw", [128, 4, NSL], F32)
    cb = sbt("cb", [128, NSL], F32)
    dtb = sbt("dtb", [128, MYSL], F32)
    dv = sbt("dv", [128, MYSL], F32)
    glb = sbt("glb", [128, 16], F32)
    a_sb = sbt("a_sb", [128, MYSL * NS], F32)
    wxp = sbt("wxp", [128, NSL, 96], BF16)
    wdt = sbt("wdt", [RK, MYSL * 128], BF16)
    hm = sbt("hm", [128, 1], F32)
    st_ln = sbt("st_ln", [8, 8, 512], F32)
    stat_ev = [None]

    blk = st.enter_context(nc.Block())

    def ld(dst, src, q="dS", **kw):
        eng = G if q in ("dG", "dW") else SY
        return s.dma(eng, q, dst, src, **kw)

    ld(ident_f[:], ident_h)
    ld(sel1[:], sel1_h.rearrange("a (c p) -> a c p", p=128))
    ld(sel8[:], sel8_h.rearrange("a (c e) -> a c e", e=8), q="dG")
    ld(rep64[:], rep64_h, q="dG")
    ld(wgate[:], w_gate_h, q="dG")
    ld(wwd[:], w_wd_h, q="dG")
    ld(wwu[:], w_wu_h, q="dG")
    ld(bu_s[:], bu_h, q="dG")
    ld(bd_s[:], bd_h)
    ld(lng[:], lng_h)
    ld(lnb[:], lnb_h)
    ld(cw[:], cw_h)
    ld(cb[:], cb_h)
    ld(dtb[:], dtb_h)
    ld(dv[:], dv_h)
    ld(glb[:], glb_h)
    ld(a_sb[:], alog_h)
    ld(hm[:], hmask_h)
    ld(wxp[:], w_xp_h, q="dG")
    ld(wdt[:], w_dt_h, q="dG")
    Vw.memset(ones_b[:], 1.0)
    for i in range(4):
        s.wait(SC, s.cur(f"dS{i}"))
    Sw.activation(out=a_sb[:], in_=a_sb[:], func=AF.Exp)
    s.wait(V, s.cur("SC"))
    Vw.tensor_scalar(a_sb[:], a_sb[:], -1.0, None, OP.mult)

    _ce = G.compute_val((G.partition_id() % 2) == 0, donate=False)
    _co = G.compute_val((G.partition_id() % 2) == 1, donate=False)

    def P_EVEN():
        return _ce

    def P_ODD():
        return _co

    def barrier():
        dma_evs = [s.cur(f"{q}{i}") for q in ("dS", "dG", "dO", "dW", "dX") for i in range(4)]
        dma_evs.append(s.cur("CC"))
        dma_evs.append(s.cur("dC0"))
        for eng in ENGINES:
            s.wait(eng, *dma_evs)
            eng.nop().then_inc(s.sems["BAR"], 1)
        s.counts["BAR"] += len(ENGINES)
        tgt = s.counts["BAR"]
        for eng in ENGINES:
            eng.wait_ge(s.sems["BAR"], tgt)
            s.waited[(id(eng), "BAR")] = tgt

    barrier()

    last_te = [None]

    def MM(*a, **k):
        last_te[0] = TE.matmul(*a, **k)
        return last_te[0]

    def TR(*a, **k):
        last_te[0] = TE.transpose(*a, **k)
        return last_te[0]

    def top():
        return s.inc(last_te[0], "T")

    def ln_ab(slot_a, slot_b):
        sx, sx2 = st_ln[:, 0, :], st_ln[:, 1, :]
        a_r, b_r = st_ln[:, slot_a, :], st_ln[:, slot_b, :]
        t6, t7 = st_ln[:, 6, :], st_ln[:, 7, :]
        s.wait(V, stat_ev[0])
        Vw.tensor_scalar(b_r, sx, 1.0 / D, None, OP.mult)
        Vw.tensor_tensor(t6, b_r, b_r, OP.mult)
        Vw.tensor_scalar(t7, sx2, 1.0 / D, None, OP.mult)
        Vw.tensor_tensor(t7, t7, t6, OP.subtract)
        Vw.tensor_scalar(t7, t7, EPS, None, OP.add)
        e = Vw.reciprocal(t6, t7)
        s.wait(SC, e)
        e2 = Sw.activation(out=a_r, in_=t6, func=AF.Sqrt)
        s.wait(V, e2)
        Vw.tensor_tensor(b_r, b_r, a_r, OP.mult)
        e3 = Vw.tensor_scalar(b_r, b_r, -1.0, None, OP.mult)
        stat_ev[0] = e3

    def make_rep(slot_a, slot_b, da, db, chunks):
        first = True
        for (row, off, w) in chunks:
            for slot, dst in ((slot_a, da), (slot_b, db)):
                bk = bank()
                s.wait(TE, *bk.r, bk.w, stat_ev[0])
                MM(bk.ap[0:128, 0:w], sel1[:, row, :], st_ln[:, slot, 0:w],
                   start=True, stop=True)
                bk.w = top()
                bk.r = []
                s.wait(V, bk.w, *(dst.r if first else []))
                e = Vw.tensor_copy(dst.t[:, off:off + w], bk.ap[0:128, 0:w])
                bk.r.append(e)
                if first:
                    dst.r = []
                dst.w = e
            first = False

    class StatAcc:
        def __init__(self, b1, b2, total):
            self.b1, self.b2 = b1, b2
            self.total = total
            self.n = 0
            self.row = 0
            self.w = 512

        def feed(self, xb_ap, x2_ap, wait_evs=()):
            first = self.n == 0
            last = self.n == self.total - 1
            if first:
                s.wait(V, *self.b1.r, self.b1.w, *self.b2.r, self.b2.w)
                Vw.memset(self.b1.ap[0:8, 0:512], 0.0)
                ez = Vw.memset(self.b2.ap[0:8, 0:512], 0.0)
                s.wait(TE, ez)
            s.wait(TE, *[e for e in wait_evs if e])
            MM(self.b1.ap[0:8, 0:self.w], sel8[:, self.row, :], xb_ap,
               start=first, stop=last)
            MM(self.b2.ap[0:8, 0:self.w], sel8[:, self.row, :], x2_ap,
               start=first, stop=last)
            ev = top()
            self.n += 1
            if last:
                self.b1.w = ev; self.b1.r = []
                self.b2.w = ev; self.b2.r = []
                s.wait(V, ev)
                Vw.tensor_copy(st_ln[:, 0, :], self.b1.ap[0:8, 0:512])
                e2 = Vw.tensor_copy(st_ln[:, 1, :], self.b2.ap[0:8, 0:512])
                self.b1.r.append(e2)
                self.b2.r.append(e2)
                stat_ev[0] = e2
            return ev

    stA = ExitStack()
    xf = Buf(sbt("xf", [128, TL], F32, stA))
    xbc_ = [Buf(sbt(f"xbc{i}", [128, 512], BF16, stA)) for i in range(2)]
    x2c_ = [Buf(sbt(f"x2c{i}", [128, 512], BF16, stA)) for i in range(2)]
    arep = Buf(sbt("arep", [128, TL], BF16, stA))
    brep = Buf(sbt("brep", [128, TL], BF16, stA))
    xn1 = [Buf(sbt(f"xn1_{i}", [128, TL], BF16, stA)) for i in range(NSD)]
    nrm = Buf(sbt("nrm", [128, TL], F32, stA))

    acc = StatAcc(psb[6], psb[7], NSD * len(CH_ALL))
    for sl in range(NSD):
        s.wait(SY, *xf.r)
        xf.w = ld(xf.ap, x_h.rearrange("(a p) t -> a p t", p=128)[sl], q="dX")
        xf.r = []
        for (row, off, w) in CH_ALL:
            xbc = xbc_[row % 2]
            x2c = x2c_[row % 2]
            acc.row, acc.w = row, w
            s.wait(SC, xf.w, *xbc.r, *x2c.r)
            e1 = Sw.activation(out=xbc.t[:, 0:w], in_=xf.t[:, off:off + w], func=AF.Copy)
            e2 = Sw.activation(out=x2c.t[:, 0:w], in_=xf.t[:, off:off + w], func=AF.Square)
            xbc.w, xbc.r = e1, []
            x2c.w, x2c.r = e2, []
            xf.r.append(e2)
            ev = acc.feed(xbc.t[:, 0:w], x2c.t[:, 0:w], (e1, e2))
            xbc.r.append(ev)
            x2c.r.append(ev)
    ln_ab(4, 5)
    make_rep(4, 5, arep, brep, CH_ALL)

    for sl in range(NSD):
        s.wait(SY, *xf.r)
        xf.w = ld(xf.ap, x_h.rearrange("(a p) t -> a p t", p=128)[sl], q="dX")
        xf.r = []
        s.wait(V, xf.w, arep.w, brep.w, *nrm.r)
        Vw.tensor_tensor(nrm.ap, xf.ap, arep.ap, OP.mult)
        e = Vw.tensor_tensor(xn1[sl].ap, nrm.ap, brep.ap, OP.add)
        nrm.w, nrm.r = e, []
        xf.r.append(e); arep.r.append(e); brep.r.append(e)
        xn1[sl].w = e

    def emit_ffn(xn_bufs, tag, evac_fn, stk, chunks, tok_total):
        ngrp = tok_total // 128
        lgf = Buf(sbt(f"lgf_{tag}", [8, tok_total], F32, stk))
        lg_tok = Buf(sbt(f"lgt_{tag}", [128, ngrp, NE], F32, stk))
        m1 = Buf(sbt(f"m1_{tag}", [128, ngrp], F32, stk))
        m2 = Buf(sbt(f"m2_{tag}", [128, ngrp], F32, stk))
        msk = Buf(sbt(f"msk_{tag}", [128, ngrp, NE], F32, stk))
        gt = Buf(sbt(f"gt_{tag}", [128, ngrp, NE], F32, stk))
        gfm = Buf(sbt(f"gfm_{tag}", [NE, tok_total], BF16, stk))
        hbuf = [Buf(sbt(f"h_{tag}{i}", [128, 512], BF16, stk)) for i in range(4)]
        first = True
        for (row, off, w) in chunks:
            bk = bank()
            s.wait(TE, *bk.r, bk.w, *[b.w for b in xn_bufs])
            for k in range(NSD):
                MM(bk.ap[0:NE, 0:w], wgate[:, k, :], xn_bufs[k].t[:, off:off + w],
                   start=(k == 0), stop=(k == NSD - 1))
            bk.w = top(); bk.r = []
            s.wait(V, bk.w, *(lgf.r if first else []))
            e = Vw.tensor_copy(lgf.t[:, off:off + w], bk.ap[0:NE, 0:w])
            bk.r.append(e)
            if first:
                lgf.r = []
            lgf.w = e
            first = False
        for (row, off, w) in chunks:
            bk = bank()
            s.wait(TE, *bk.r, bk.w, lgf.w)
            ng = w // 128
            for g in range(ng):
                TR(bk.ap[0:128, g * 8:(g + 1) * 8],
                   lgf.t[:, off + g * 128: off + (g + 1) * 128],
                   ident_f[0:8, 0:8])
            bk.w = top(); bk.r = []
            s.wait(V, bk.w)
            g0 = off // 128
            e = Vw.tensor_copy(
                lg_tok.t[:, g0:g0 + ng, :],
                bk.ap[0:128, 0:ng * 8].rearrange("p (a b) -> p a b", b=8))
            bk.r.append(e)
            lg_tok.w = e
        s.wait(V, lg_tok.w)
        Vw.tensor_reduce(m1.ap, lg_tok.ap, AX.X, OP.max)
        Vw.tensor_tensor(msk.ap, lg_tok.ap,
                         m1.ap.unsqueeze(2).broadcast_to([128, ngrp, NE]), OP.is_equal)
        Vw.scalar_tensor_tensor(msk.ap, msk.ap, -1e30, lg_tok.ap, OP.mult, OP.add)
        Vw.tensor_reduce(m2.ap, msk.ap, AX.X, OP.max)
        e = Vw.tensor_tensor(gt.ap, lg_tok.ap,
                             m1.ap.unsqueeze(2).broadcast_to([128, ngrp, NE]), OP.subtract)
        s.wait(SC, e)
        e2 = Sw.activation(out=gt.ap, in_=gt.ap, func=AF.Exp)
        s.wait(V, e2)
        Vw.tensor_tensor(msk.ap, lg_tok.ap,
                         m2.ap.unsqueeze(2).broadcast_to([128, ngrp, NE]), OP.is_ge)
        Vw.tensor_tensor(gt.ap, gt.ap, msk.ap, OP.mult)
        e3 = Vw.tensor_tensor(m2.ap, m2.ap, m1.ap, OP.subtract)
        s.wait(SC, e3)
        e4 = Sw.activation(out=m2.ap, in_=m2.ap, func=AF.Exp)
        s.wait(V, e4)
        Vw.tensor_scalar(m2.ap, m2.ap, 1.0, None, OP.add)
        Vw.reciprocal(m2.ap, m2.ap)
        gt.w = Vw.tensor_tensor(gt.ap, gt.ap,
                                m2.ap.unsqueeze(2).broadcast_to([128, ngrp, NE]), OP.mult)
        first = True
        for (row, off, w) in chunks:
            bk = bank()
            s.wait(TE, *bk.r, bk.w, gt.w)
            ng = w // 128
            g0 = off // 128
            for g in range(ng):
                TR(bk.ap[0:8, g * 128:(g + 1) * 128],
                   gt.t[:, g0 + g, :], ident_f[:])
            bk.w = top(); bk.r = []
            s.wait(V, bk.w, *(gfm.r if first else []))
            e = Vw.tensor_copy(gfm.t[:, off:off + w], bk.ap[0:8, 0:w])
            bk.r.append(e)
            if first:
                gfm.r = []
            gfm.w = e
            first = False
        for (row, off, w) in chunks:
            for oc in range(4):
                bk = bank()
                s.wait(TE, *bk.r, bk.w)
                for k in range(NSD):
                    MM(bk.ap[0:128, 0:w], wwd[:, k, oc * 128:(oc + 1) * 128],
                       xn_bufs[k].t[:, off:off + w],
                       start=(k == 0), stop=(k == NSD - 1))
                bk.w = top(); bk.r = []
                s.wait(SC, bk.w, *hbuf[oc].r)
                e = Sw.activation(out=hbuf[oc].t[:, 0:w], in_=bk.ap[0:128, 0:w],
                                  func=AF.Relu, bias=bd_s[:, oc:oc + 1])
                bk.r.append(e)
                hbuf[oc].w, hbuf[oc].r = e, []
            for oc in range(4):
                bk = bank()
                s.wait(TE, *bk.r, bk.w, gfm.w)
                MM(bk.ap[0:128, 0:w], rep64[:, oc * 128:(oc + 1) * 128],
                   gfm.t[:, off:off + w], start=True, stop=True)
                bk.w = top(); bk.r = []
                s.wait(V, bk.w, hbuf[oc].w)
                e = Vw.tensor_tensor(hbuf[oc].t[:, 0:w], hbuf[oc].t[:, 0:w],
                                     bk.ap[0:128, 0:w], OP.mult)
                bk.r.append(e)
                hbuf[oc].w = e
            for oc in range(NSD):
                bk = bank()
                s.wait(TE, *bk.r, bk.w, *[b.w for b in hbuf], gfm.w)
                for k in range(4):
                    MM(bk.ap[0:128, 0:w], wwu[:, k, oc * 128:(oc + 1) * 128],
                       hbuf[k].t[:, 0:w], start=(k == 0), stop=False)
                MM(bk.ap[0:128, 0:w], bu_s[:, oc * 128:(oc + 1) * 128],
                   gfm.t[:, off:off + w], start=False, stop=True)
                bk.w = top(); bk.r = []
                for hb in hbuf:
                    hb.r.append(bk.w)
                evac_fn(oc, row, off, w, bk)

    r1c = [Buf(sbt(f"r1c{i}", [128, 512], F32, stA)) for i in range(2)]
    r1b = [Buf(sbt(f"r1b{i}", [128, 512], BF16, stA)) for i in range(2)]
    r1q = [Buf(sbt(f"r1q{i}", [128, 512], BF16, stA)) for i in range(2)]
    xre = [Buf(sbt(f"xre{i}", [128, 512], F32, stA)) for i in range(2)]
    acc2 = StatAcc(psb[6], psb[7], len(CH_ALL) * NSD)
    _f1n = [0]
    _f1pend = [None]

    def _feed_pending(acc):
        p = _f1pend[0]
        if p is None:
            return
        row, rb, rq, c1, c2 = p
        acc.row, acc.w = row, 512
        ev = acc.feed(rb.ap, rq.ap, (c1, c2))
        rb.r.append(ev)
        rq.r.append(ev)
        _f1pend[0] = None

    def evac_ffn1(oc, row, off, w, bk):
        i = _f1n[0] % 2
        _f1n[0] += 1
        xr, rc, rb, rq = xre[i], r1c[i], r1b[i], r1q[i]
        s.wait(SY, *xr.r)
        xr.w = ld(xr.t[:, 0:w], x_h.rearrange("(a p) t -> a p t", p=128)[oc, :, off:off + w], q="dX")
        xr.r = []
        s.wait(V, bk.w, xr.w, *rc.r)
        e = Vw.scalar_tensor_tensor(rc.t[:, 0:w], bk.ap[0:128, 0:w], 0.25,
                                    xr.t[:, 0:w], OP.mult, OP.add)
        bk.r.append(e); xr.r.append(e)
        rc.w, rc.r = e, []
        s.wait(G, rc.w)
        e2 = s.dma(G, "dG", r1_h.rearrange("(a p) t -> a p t", p=128)[oc, :, off:off + w],
                   rc.t[:, 0:w])
        rc.r.append(e2)
        s.wait(SC, rc.w, *rb.r, *rq.r)
        c1 = Sw.activation(out=rb.t[:, 0:w], in_=rc.t[:, 0:w], func=AF.Copy)
        c2 = Sw.activation(out=rq.t[:, 0:w], in_=rc.t[:, 0:w], func=AF.Square)
        rb.w, rb.r = c1, []
        rq.w, rq.r = c2, []
        rc.r.append(c2)
        _feed_pending(acc2)
        _f1pend[0] = (row, rb, rq, c1, c2)

    emit_ffn(xn1, "f1", evac_ffn1, stA, CH_ALL, TL)
    _feed_pending(acc2)
    barrier()
    ln_ab(4, 5)
    t6, t7 = st_ln[:, 6, :], st_ln[:, 7, :]
    s.wait(V, stat_ev[0])
    Vw.tensor_tensor(t6, st_ln[:, 4, :], st_ln[:, 4, :], OP.mult)
    e = Vw.reciprocal(t7, t6)
    s.wait(V, e)
    Vw.tensor_scalar(t7, t7, -EPS, None, OP.add)
    Vw.tensor_tensor(t7, t7, t6, OP.mult)
    Vw.tensor_scalar(t7, t7, EPS, None, OP.add)
    e2 = Vw.reciprocal(t7, t7)
    s.wait(SC, e2)
    e3 = Sw.activation(out=t7, in_=t7, func=AF.Sqrt)
    s.wait(V, e3)
    Vw.tensor_tensor(st_ln[:, 2, :], st_ln[:, 4, :], t7, OP.mult)
    e4 = Vw.tensor_tensor(st_ln[:, 3, :], st_ln[:, 5, :], t7, OP.mult)
    stat_ev[0] = e4
    barrier()
    stA.close()
    if stop == "A":
        st.close()
        return nc

    stB = ExitStack()
    xf = Buf(sbt("xfB", [128, TL], F32, stB))
    nrmB = Buf(sbt("nrmB", [128, TL], F32, stB))
    arepB = Buf(sbt("arepB", [128, TL], BF16, stB))
    brepB = Buf(sbt("brepB", [128, TL], BF16, stB))
    xn2i = [Buf(sbt(f"xn2i{i}", [128, TL], BF16, stB)) for i in range(NSD)]
    wipc = [Buf(sbt(f"wipc{i}", [128, NSD, 256], BF16, stB)) for i in range(2)]
    xp_sl = Buf(sbt("xp_sl", [128, TL], BF16, stB))
    cbacc = Buf(sbt("cbacc", [128, TL], BF16, stB))
    xc_sl = [Buf(sbt(f"xc_sl{i}", [128, TL], BF16, stB)) for i in range(2)]
    zc = [Buf(sbt(f"zcB{i}", [128, 512], BF16, stB)) for i in range(2)]
    dtlr = Buf(sbt("dtlrB", [96, TR_], BF16, stB))

    make_rep(2, 3, arepB, brepB, CH_ALL)

    for sl in range(NSD):
        s.wait(SY, *xf.r)
        xf.w = ld(xf.ap, r1_h.rearrange("(a p) t -> a p t", p=128)[sl], q="dX")
        xf.r = []
        s.wait(V, xf.w, arepB.w, brepB.w, *nrmB.r)
        Vw.tensor_tensor(nrmB.ap, xf.ap, arepB.ap, OP.mult)
        e = Vw.tensor_tensor(nrmB.ap, nrmB.ap, brepB.ap, OP.add)
        nrmB.w, nrmB.r = e, []
        xf.r.append(e); arepB.r.append(e); brepB.r.append(e)
        s.wait(SC, e, *xn2i[sl].r)
        e2 = Sw.activation(out=xn2i[sl].ap, in_=nrmB.ap, func=AF.Identity,
                           scale=lng[:, sl:sl + 1], bias=lnb[:, sl:sl + 1])
        nrmB.r.append(e2)
        xn2i[sl].w, xn2i[sl].r = e2, []

    def rebind(old_buf, tensor):
        b = Buf(tensor)
        b.r = list(old_buf.r) + ([old_buf.w] if old_buf.w else [])
        return b

    xdbl = rebind(xf, xf.t)
    cacc = rebind(nrmB, nrmB.t)
    s.wait(V, *xdbl.r)
    ezero = Vw.memset(xdbl.t[0:96, R0:R0 + TR_], 0.0)
    xdbl.w, xdbl.r = ezero, []

    def _xproj(psl):
        xcb = xc_sl[psl % 2]
        for (row, off, w) in CH_REAL:
            bk = bank()
            s.wait(TE, *bk.r, bk.w, xcb.w)
            MM(bk.ap[0:96, 0:w], wxp[:, psl, :], xcb.t[:, off:off + w],
               start=True, stop=True)
            bk.w = top(); bk.r = []
            xcb.r.append(bk.w)
            s.wait(V, bk.w, xdbl.w)
            e = Vw.tensor_tensor(xdbl.t[0:96, off:off + w], xdbl.t[0:96, off:off + w],
                                 bk.ap[0:96, 0:w], OP.add)
            bk.r.append(e)
            xdbl.w = e

    wip_r = w_ip_h.rearrange("(k p) m -> p k m", p=128)
    for sl in range(NSL):
        wb = wipc[sl % 2]
        s.wait(G, *wb.r)
        wb.w = ld(wb.t[:, :, 0:128], wip_r[:, :, sl * 128:(sl + 1) * 128], q="dW")
        wb.r = []
        for (row, off, w) in CH_ALL:
            bk = bank()
            s.wait(TE, *bk.r, bk.w, wb.w, *[b.w for b in xn2i])
            for k in range(NSD):
                MM(bk.ap[0:128, 0:w], wb.t[:, k, 0:128],
                   xn2i[k].t[:, off:off + w], start=(k == 0), stop=(k == NSD - 1))
            bk.w = top(); bk.r = []
            wb.r.append(bk.w)
            s.wait(SC, bk.w, *(xp_sl.r if row == 0 else []))
            e = Sw.activation(out=xp_sl.t[:, off:off + w], in_=bk.ap[0:128, 0:w],
                              func=AF.Copy)
            bk.r.append(e)
            if row == 0:
                xp_sl.r = []
            xp_sl.w = e
        s.wait(V, xp_sl.w)
        ehm = Vw.tensor_scalar(xp_sl.t[:, R0 - 3:R0], xp_sl.t[:, R0 - 3:R0],
                               hm[:, 0:1], None, OP.mult)
        xp_sl.w = ehm
        s.wait(V, xp_sl.w, *cbacc.r, *cacc.r)
        cbacc.r = []
        cacc.r = []
        Vw.tensor_scalar(cbacc.ap, xp_sl.ap, cw[:, 3, sl:sl + 1], None, OP.mult)
        Vw.scalar_tensor_tensor(cbacc.t[:, 1:TL], xp_sl.t[:, 0:TL - 1], cw[:, 2, sl:sl + 1],
                                cbacc.t[:, 1:TL], OP.mult, OP.add)
        Vw.scalar_tensor_tensor(cbacc.t[:, 2:TL], xp_sl.t[:, 0:TL - 2], cw[:, 1, sl:sl + 1],
                                cbacc.t[:, 2:TL], OP.mult, OP.add)
        ec = Vw.scalar_tensor_tensor(cacc.t[:, 3:TL], xp_sl.t[:, 0:TL - 3], cw[:, 0, sl:sl + 1],
                                     cbacc.t[:, 3:TL], OP.mult, OP.add)
        cacc.w, cacc.r = ec, []
        cbacc.w = ec
        cbacc.r.append(ec)
        xp_sl.r.append(ec)
        xcb = xc_sl[sl % 2]
        s.wait(SC, ec, *xcb.r)
        exc = Sw.activation(out=xcb.t[:, 3:TL], in_=cacc.t[:, 3:TL], func=AF.Silu,
                            bias=cb[:, sl:sl + 1])
        cacc.r.append(exc)
        xcb.w, xcb.r = exc, []
        s.wait(G, xcb.w)
        e_loc = s.dma(G, "dG", xc_all.rearrange("(a p) t -> a p t", p=128)[sl],
                      xcb.t[:, R0:R0 + TR_])
        xcb.r.append(e_loc)
        if sl > 0:
            _xproj(sl - 1)
        if sl == NSL - 1:
            _xproj(sl)
    s.wait(SC, xdbl.w, *dtlr.r)
    ecast = Sw.activation(out=dtlr.ap, in_=xdbl.t[0:96, R0:R0 + TR_], func=AF.Copy)
    dtlr.w, dtlr.r = ecast, []
    xdbl.r.append(ecast)
    s.wait(G, dtlr.w)
    e_sp = s.dma(G, "dG", xdbl_src, dtlr.ap)
    dtlr.r.append(e_sp)

    s.wait(G, *[s.cur(f"dG{i}") for i in range(4)], s.cur("dC0"))
    _est = s.dma(G, "dG", xc_src, xc_all[MYSL * 128:DI, :])
    s.wait(G, _est)
    s.dma(G, "dG", xc_src, xc_all[0:MYSL * 128, :], cond=P_ODD())
    _exc1 = s.dma(G, "dG", xcs_h[:, 0:TR_], xc_all[0:MYSL * 128, :])
    _exc2 = s.dma(G, "dG", xcs_h[:, TR_:TSEQ], xc_all[MYSL * 128:DI, :])
    s.wait(G, *[s.cur(f"dG{i}") for i in range(4)], s.cur("dC0"))
    cc1 = s.inc(G.collective_compute("AllGather", OP.bypass,
                                     replica_groups=[[0, 1], [2, 3], [4, 5], [6, 7]],
                                     ins=[xdbl_src], outs=[xdbl_g]), "CC", 1)
    cc2 = s.inc(G.collective_compute("AllGather", OP.bypass,
                                     replica_groups=[[0, 1], [2, 3], [4, 5], [6, 7]],
                                     ins=[xc_src], outs=[xc_g]), "CC", 1)
    wipz = Buf(sbt("wipz", [128, NSD, DI], BF16, stB))
    wipz.w = ld(wipz.ap, wip_r[:, :, DI:2 * DI], q="dW")
    for sl in range(NSL):
        for ci, (row, off, w) in enumerate(CH_REAL):
            zcb = zc[ci % 2]
            bk = bank()
            s.wait(TE, *bk.r, bk.w, wipz.w)
            for k in range(NSD):
                MM(bk.ap[0:128, 0:w], wipz.t[:, k, sl * 128:(sl + 1) * 128],
                   xn2i[k].t[:, off:off + w], start=(k == 0), stop=(k == NSD - 1))
            bk.w = top(); bk.r = []
            s.wait(SC, bk.w, *zcb.r)
            e = Sw.activation(out=zcb.t[:, 0:w], in_=bk.ap[0:128, 0:w], func=AF.Silu)
            bk.r.append(e)
            zcb.w, zcb.r = e, []
            s.wait(G, zcb.w)
            lo = off - R0
            e_loc = s.dma(G, "dG", sz_all.rearrange("(a p) t -> a p t", p=128)[sl, :, lo:lo + w],
                          zcb.t[:, 0:w])
            zcb.r.append(e_loc)
    sz_spill_evs = [s.cur(f"dG{i}") for i in range(4)]
    s.wait(G, cc1, cc2)
    _exd1 = s.dma(G, "dG", xdblf_h[:, 0:TR_], dtlr.ap)
    _exd2 = s.dma(G, "dG", xdblf_h[:, TR_:TSEQ], dtlr.ap)
    s.wait(G, _exd1, _exd2)
    s.dma(G, "dG", xdblf_h[:, TR_:TSEQ], xdbl_g[1], cond=P_EVEN())
    e_xd = s.dma(G, "dG", xdblf_h[:, 0:TR_], xdbl_g[0], cond=P_ODD())
    dtlr.r.append(e_xd)
    dtlr.r.append(_exd1)
    dtlr.r.append(_exd2)
    s.wait(G, _exc1, _exc2)
    s.dma(G, "dG", xcs_h[:, TR_:TSEQ], xc_g[1], cond=P_EVEN())
    s.dma(G, "dG", xcs_h[:, 0:TR_], xc_g[0], cond=P_ODD())
    barrier()
    stB.close()
    if stop == "B1":
        st.close()
        return nc

    stS = ExitStack()
    dtlr2s = [Buf(sbt(f"dtlr2{i}", [RK, TCS], BF16, stS)) for i in range(2)]
    b0s = [Buf(sbt(f"ball0{i}", [128, 8, TCS], BF16, stS)) for i in range(2)]
    ball1 = Buf(sbt("ball1", [128, 8, TCS], BF16, stS))
    call = [Buf(sbt(f"call{i}", [128, 8, TCS], BF16, stS)) for i in range(2)]
    dtt = [Buf(sbt(f"dtt{i}", [128, TCS], F32, stS)) for i in range(2)]
    sp1 = Buf(sbt("sp1", [128, TCS], F32, stS))
    dtu = [Buf(sbt(f"dtu{i}", [128, TCS], BF16, stS)) for i in range(2)]
    xcs = [Buf(sbt(f"xcs{i}", [128, TCS], BF16, stS)) for i in range(2)]
    dA = [Buf(sbt(f"dA{i}", [128, TCS], F32, stS)) for i in range(2)]
    qall = Buf(sbt("qall", [128, NS // 2, TCS], BF16, stS))
    hbig = Buf(sbt("hbig", [128, NS // 2, TCS], BF16, stS))
    yv = Buf(sbt("yv", [128, TCS], F32, stS))
    ygb = Buf(sbt("ygb", [128, TCS], BF16, stS))
    state = Buf(sbt("state", [128, MYSL * NS], F32, stS))

    state.w = Vw.memset(state.ap, 0.0)

    def _bcld(dst, r0_, c0_):
        return s.dma(G, "dG", dst,
                     xdblf_h[RK + r0_:RK + r0_ + 4, c0_:c0_ + TCS]
                     .unsqueeze(0).broadcast_to([128, 4, TCS]))

    yg_pend = []

    def _fill_yg(k, r0_, r1_, ccy):
        _own = ygf_h[r0_:r1_, :]
        _par = ygf_h[MYSL * 128 + r0_:MYSL * 128 + r1_, :]
        s.wait(G, ccy)
        _ea = s.dma(G, "dG", _own, yg_gs[k][0][:, 0:TR_])
        _eb = s.dma(G, "dG", _par, yg_gs[k][1][:, 0:TR_])
        s.wait(G, _ea, _eb)
        s.dma(G, "dG", _own, yg_gs[k][0][:, TR_:TSEQ], cond=P_ODD())
        s.dma(G, "dG", _par, yg_gs[k][1][:, TR_:TSEQ], cond=P_ODD())

    eo3 = []
    for tc in range(NSC):
        c0 = tc * TCS
        dtlr2 = dtlr2s[tc % 2]
        ball = [b0s[tc % 2], ball1]
        if tc == 0:
            s.wait(SY, *dtlr2.r)
            dtlr2.w = ld(dtlr2.ap, xdblf_h[0:RK, c0:c0 + TCS], q="dX")
            dtlr2.r = []
            s.wait(G, *ball[0].r)
            ball[0].w = (_bcld(ball[0].t[:, 0:4, :], 0, c0),
                         _bcld(ball[0].t[:, 4:8, :], 4, c0))
            ball[0].r = []
        s.wait(G, *ball[1].r, *call[0].r, *call[1].r)
        ball[1].w = (_bcld(ball[1].t[:, 0:4, :], 8, c0),
                     _bcld(ball[1].t[:, 4:8, :], 12, c0))
        call[0].w = (_bcld(call[0].t[:, 0:4, :], NS + 0, c0),
                     _bcld(call[0].t[:, 4:8, :], NS + 4, c0))
        call[1].w = (_bcld(call[1].t[:, 0:4, :], NS + 8, c0),
                     _bcld(call[1].t[:, 4:8, :], NS + 12, c0))
        ball[1].r = []
        call[0].r = []
        call[1].r = []
        def dt_chain(sl, c0_, dtl):
            dtb_ = dtt[sl % 2]
            dtu_ = dtu[sl % 2]
            xcb_ = xcs[sl % 2]
            s.wait(SY, *xcb_.r)
            xcb_.w = ld(xcb_.ap, xcs_h.rearrange("(a p) t -> a p t", p=128)[sl, :, c0_:c0_ + TCS], q="dX")
            xcb_.r = []
            for hh in range(TCS // 512):
                bk = bank()
                s.wait(TE, *bk.r, bk.w, dtl.w)
                MM(bk.ap[0:128, 0:512], wdt[:, sl * 128:(sl + 1) * 128],
                   dtl.t[:, hh * 512:(hh + 1) * 512],
                   start=True, stop=True)
                dtl.r.append(s.cur("T"))
                bk.w = top(); bk.r = []
                s.wait(SC, bk.w, *(dtb_.r if hh == 0 else []))
                e = Sw.activation(out=dtb_.t[:, hh * 512:(hh + 1) * 512],
                                  in_=bk.ap[0:128, 0:512], func=AF.Identity,
                                  bias=dtb[:, sl:sl + 1])
                bk.r.append(e)
                if hh == 0:
                    dtb_.r = []
                dtb_.w = e
            s.wait(V, dtb_.w, *sp1.r)
            e = Vw.tensor_scalar_min(sp1.ap, dtb_.ap, 20.0)
            sp1.w, sp1.r = e, []
            s.wait(SC, e)
            Sw.activation(out=sp1.ap, in_=sp1.ap, func=AF.Exp)
            e3 = Sw.activation(out=sp1.ap, in_=sp1.ap, func=AF.Ln, bias=1.0)
            sp1.w = e3
            s.wait(V, e3)
            e4 = Vw.tensor_tensor(dtb_.ap, dtb_.ap, sp1.ap, OP.max)
            dtb_.w = e4
            sp1.r.append(e4)
            s.wait(V, dtb_.w, xcb_.w, *dtu_.r)
            e5 = Vw.tensor_tensor(dtu_.ap, dtb_.ap, xcb_.ap, OP.mult)
            dtu_.w, dtu_.r = e5, []
            xcb_.r.append(e5)

        if tc == 0:
            dt_chain(0, c0, dtlr2)
        for sl in range(MYSL):
            if sl + 1 < MYSL:
                dt_chain(sl + 1, c0, dtlr2)
            elif tc + 1 < NSC:
                dt_chain(0, c0 + TCS, dtlr2s[(tc + 1) % 2])
            if sl == 5 and tc + 1 < NSC:
                nb = dtlr2s[(tc + 1) % 2]
                s.wait(SY, *nb.r)
                nb.w = ld(nb.ap, xdblf_h[0:RK, c0 + TCS:c0 + 2 * TCS], q="dX")
                nb.r = []
                n0b = b0s[(tc + 1) % 2]
                s.wait(G, *n0b.r)
                n0b.w = (_bcld(n0b.t[:, 0:4, :], 0, c0 + TCS),
                         _bcld(n0b.t[:, 4:8, :], 4, c0 + TCS))
                n0b.r = []
            dtt_c = dtt[sl % 2]
            dtu_c = dtu[sl % 2]
            xcs_c = xcs[sl % 2]
            for half in range(2):
                n0 = half * 8
                s.wait(V, dtu_c.w, ball[half].w, *qall.r)
                eqa = Vw.tensor_tensor(qall.ap,
                                       dtu_c.ap.unsqueeze(1).broadcast_to([128, 8, TCS]),
                                       ball[half].ap, OP.mult)
                qall.w, qall.r = eqa, []
                dtu_c.r.append(eqa)
                ball[half].r.append(eqa)
                eh = None
                for n in range(8):
                    da = dA[n % 2]
                    s.wait(SC, dtt_c.w, *da.r)
                    ea = Sw.activation(out=da.ap, in_=dtt_c.ap, func=AF.Exp,
                                       scale=a_sb[:, sl * NS + n0 + n: sl * NS + n0 + n + 1])
                    da.w, da.r = ea, []
                    col = sl * NS + n0 + n
                    s.wait(V, da.w, qall.w)
                    eh = Vw.tensor_tensor_scan(hbig.t[:, n, :], da.ap, qall.t[:, n, :],
                                               state.t[:, col:col + 1], OP.mult, OP.add)
                    da.r.append(eh)
                qall.r.append(eh)
                dtt_c.r.append(eh)
                Vw.tensor_copy(state.t[:, sl * NS + n0: sl * NS + n0 + 8],
                               hbig.t[:, :, TCS - 1:TCS].rearrange("p a b -> p (a b)"))
                s.wait(V, call[half].w)
                hc = qall.t
                Vw.tensor_tensor(hc[:, 0:8, :], hbig.t[:, 0:8, :],
                                 call[half].ap, OP.mult)
                Vw.tensor_tensor(hc[:, 0:4, :], hc[:, 0:4, :], hc[:, 4:8, :], OP.add)
                Vw.tensor_tensor(hc[:, 0:2, :], hc[:, 0:2, :], hc[:, 2:4, :], OP.add)
                if half == 0:
                    e6 = Vw.tensor_tensor(yv.ap, hc[:, 0, :], hc[:, 1, :], OP.add)
                else:
                    Vw.tensor_tensor(sp1.ap, hc[:, 0, :], hc[:, 1, :], OP.add)
                    e6 = Vw.tensor_tensor(yv.ap, yv.ap, sp1.ap, OP.add)
                    sp1.r.append(e6)
                call[half].r.append(e6)
                qall.r.append(e6)
            for _ in range(2):
                bkw = bank()
                s.wait(TE, *bkw.r, bkw.w, s.cur("V"))
                MM(bkw.ap[0:64, 0:64], wdt[:, sl * 128:sl * 128 + 64],
                   dtlr2.t[:, 0:64], start=True, stop=True)
                bkw.w = top(); bkw.r = []
                dtlr2.r.append(bkw.w)
            e8 = Vw.scalar_tensor_tensor(yv.ap, xcs_c.ap, dv[:, sl:sl + 1], yv.ap,
                                         OP.mult, OP.add)
            xcs_c.r.append(e8)
            s.wait(SC, e8, *ygb.r)
            e9 = Sw.activation(out=ygb.ap, in_=yv.ap, func=AF.Copy)
            yv.r.append(e9)
            ygb.w, ygb.r = e9, []
            s.wait(G, ygb.w)
            eo = s.dma(G, "dG", yg_h.rearrange("(a p) t -> a p t", p=128)[sl, :, c0:c0 + TCS],
                       ygb.ap)
            ygb.r.append(eo)
            if tc == NSC - 1:
                eo3.append(eo)
                if sl % 2 == 1:
                    k = sl // 2
                    r0_, r1_ = (sl - 1) * 128, (sl + 1) * 128
                    s.wait(G, eo3[sl - 1], eo3[sl])
                    est = s.dma(G, "dG", yg_srcs[k], yg_h[r0_:r1_, :])
                    s.wait(G, est)
                    ccy = s.inc(G.collective_compute(
                        "AllGather", OP.bypass,
                        replica_groups=[[0, 1], [2, 3], [4, 5], [6, 7]],
                        ins=[yg_srcs[k]], outs=[yg_gs[k]]), "CC", 1)
                    if yg_pend:
                        _fill_yg(*yg_pend.pop())
                    yg_pend.append((k, r0_, r1_, ccy))
    while yg_pend:
        _fill_yg(*yg_pend.pop())
    barrier()
    stS.close()
    if stop == "B2":
        st.close()
        return nc

    stC = ExitStack()
    mtl = [Buf(sbt(f"mtl{i}", [128, TR_], BF16, stC)) for i in range(NSD)]

    stC1 = ExitStack()
    wopc = [Buf(sbt(f"wopc{i}", [128, NSL, 128], BF16, stC1)) for i in range(2)]
    ygc = [Buf(sbt(f"ygc{i}", [128, NSL, 512], BF16, stC1)) for i in range(2)]
    szc1 = Buf(sbt("szc", [128, NSL, 512], BF16, stC1))
    wrr = w_op_h.rearrange("(k p) m -> p k m", p=128)
    _wn = [0]

    def _wopld(oc):
        b = wopc[_wn[0] % 2]
        _wn[0] += 1
        s.wait(G, *b.r)
        b.w = ld(b.ap, wrr[:, :, oc * 128:(oc + 1) * 128], q="dW")
        b.r = []
        return b

    def emit_c1_chunk(row, off, w):
        yb = ygc[row % 2]
        zb = szc1
        s.wait(SY, *yb.r, *zb.r)
        yb.w = ld(yb.ap, ygf_h.rearrange("(k p) t -> p k t", p=128)[:, :, off:off + w], q="dX")
        zb.w = ld(zb.ap, sz_all.rearrange("(k p) t -> p k t", p=128)[:, :, off:off + w], q="dX")
        yb.r = []
        zb.r = []
        s.wait(V, yb.w, zb.w)
        eg = Vw.tensor_tensor(yb.ap, yb.ap, zb.ap, OP.mult)
        yb.w = eg
        zb.r.append(eg)
        wb = _wopld(0)
        for oc in range(NSD):
            wnext = _wopld(oc + 1) if oc + 1 < NSD else None
            bk = bank()
            s.wait(TE, *bk.r, bk.w, wb.w, yb.w)
            for k in range(NSL):
                MM(bk.ap[0:128, 0:w], wb.t[:, k, :],
                   yb.t[:, k, :], start=(k == 0), stop=(k == NSL - 1))
            bk.w = top(); bk.r = []
            yb.r.append(bk.w)
            wb.r.append(bk.w)
            s.wait(SC, bk.w)
            e = Sw.activation(out=mtl[oc].t[:, off:off + w],
                              in_=bk.ap[0:128, 0:w], func=AF.Gelu)
            bk.r.append(e)
            mtl[oc].w = e
            if wnext is not None:
                wb = wnext

    stC2 = ExitStack()
    a2rep = Buf(sbt("a2rep", [128, TR_], F32, stC2))
    b2rep = Buf(sbt("b2rep", [128, TR_], F32, stC2))
    make_rep(4, 5, a2rep, b2rep, CH_REAL_SH := [(1 + i, 512 * i, 512) for i in range(4)])
    wgl = Buf(sbt("wgl", [128, NSD, DI], BF16, stC2))
    sgc = [Buf(sbt(f"sgc{i}", [128, 512], BF16, stC2)) for i in range(2)]
    xmc = [Buf(sbt(f"xmc{i}", [128, 512], F32, stC2)) for i in range(2)]
    xmb = [Buf(sbt(f"xmb{i}", [128, 512], BF16, stC2)) for i in range(2)]
    xmq = [Buf(sbt(f"xmq{i}", [128, 512], BF16, stC2)) for i in range(2)]
    rre = [Buf(sbt(f"rre{i}", [128, 512], F32, stC2)) for i in range(2)]
    wgl.w = ld(wgl.ap, w_gl_h.rearrange("(k p) m -> p k m", p=128), q="dW")
    acc3 = StatAcc(psb[6], psb[7], len(CH_C) * NSD)
    _c2n = [0]
    _c2pend = [None]

    def _feed_pending3():
        p = _c2pend[0]
        if p is None:
            return
        row, rb, rq, c1, c2 = p
        acc3.row, acc3.w = row, 512
        ev = acc3.feed(rb.ap, rq.ap, (c1, c2))
        rb.r.append(ev)
        rq.r.append(ev)
        _c2pend[0] = None

    for (row, off, w) in CH_C:
        emit_c1_chunk(row, off, w)
        for sl in range(NSD):
            i = _c2n[0] % 2
            _c2n[0] += 1
            sg, xm, xb_, xq_, rr_ = sgc[i], xmc[i], xmb[i], xmq[i], rre[i]
            s.wait(SY, *rr_.r)
            rr_.w = ld(rr_.ap, r1_h.rearrange("(a p) t -> a p t", p=128)[sl, :, R0 + off:R0 + off + w], q="dX")
            rr_.r = []
            bk1 = bank(); bk2 = bank()
            s.wait(TE, *bk1.r, bk1.w, *bk2.r, bk2.w, wgl.w, *[b.w for b in mtl])
            for k in range(NSD):
                MM(bk1.ap[0:128, 0:w], wgl.t[:, k, sl * 128:(sl + 1) * 128],
                   mtl[k].t[:, off:off + w], start=(k == 0), stop=(k == NSD - 1))
            for k in range(NSD):
                MM(bk2.ap[0:128, 0:w], wgl.t[:, k, D + sl * 128: D + (sl + 1) * 128],
                   mtl[k].t[:, off:off + w], start=(k == 0), stop=(k == NSD - 1))
            evt = top()
            bk1.w = evt; bk1.r = []
            bk2.w = evt; bk2.r = []
            s.wait(SC, evt, *sg.r)
            e = Sw.activation(out=sg.ap, in_=bk2.ap[0:128, 0:512], func=AF.Sigmoid,
                              bias=glb[:, 8 + sl:9 + sl])
            bk2.r.append(e)
            sg.w, sg.r = e, []
            s.wait(V, e, bk1.w, rr_.w, a2rep.w, b2rep.w, *xm.r)
            Vw.scalar_tensor_tensor(xm.ap, bk1.ap[0:128, 0:512], glb[:, sl:sl + 1],
                                    sg.ap, OP.add, OP.mult)
            Vw.tensor_tensor(xm.ap, xm.ap, rr_.ap, OP.add)
            Vw.tensor_tensor(xm.ap, xm.ap, b2rep.t[:, off:off + w], OP.add)
            Vw.tensor_tensor(rr_.ap, rr_.ap, a2rep.t[:, off:off + w], OP.mult)
            e2 = Vw.tensor_tensor(xm.ap, xm.ap, rr_.ap, OP.add)
            rr_.r.append(e2); a2rep.r.append(e2); b2rep.r.append(e2)
            sg.r.append(e2); bk1.r.append(e2)
            xm.w, xm.r = e2, []
            s.wait(G, xm.w)
            ed = s.dma(G, "dG", xmid_h.rearrange("(a p) t -> a p t", p=128)[sl, :, off:off + w],
                       xm.ap)
            xm.r.append(ed)
            s.wait(SC, xm.w, *xb_.r, *xq_.r)
            c1 = Sw.activation(out=xb_.ap, in_=xm.ap, func=AF.Copy)
            c2 = Sw.activation(out=xq_.ap, in_=xm.ap, func=AF.Square)
            xb_.w, xb_.r = c1, []
            xq_.w, xq_.r = c2, []
            xm.r.append(c2)
            _feed_pending3()
            _c2pend[0] = (row, xb_, xq_, c1, c2)
    _feed_pending3()
    barrier()
    stC2.close()
    stC1.close()
    stC.close()
    if stop == "C2":
        st.close()
        return nc

    stC3 = ExitStack()
    ln_ab(2, 3)
    arepC = Buf(sbt("arepC", [128, TR_], BF16, stC3))
    brepC = Buf(sbt("brepC", [128, TR_], BF16, stC3))
    make_rep(2, 3, arepC, brepC, CH_C)
    xn3 = [Buf(sbt(f"xn3_{i}", [128, TR_], BF16, stC3)) for i in range(NSD)]
    xfC = Buf(sbt("xfC", [128, TR_], F32, stC3))
    nrmC = Buf(sbt("nrmC", [128, TR_], F32, stC3))
    for sl in range(NSD):
        s.wait(SY, *xfC.r)
        xfC.w = ld(xfC.ap, xmid_h.rearrange("(a p) t -> a p t", p=128)[sl], q="dX")
        xfC.r = []
        s.wait(V, xfC.w, arepC.w, brepC.w, *nrmC.r)
        Vw.tensor_tensor(nrmC.ap, xfC.ap, arepC.ap, OP.mult)
        e = Vw.tensor_tensor(xn3[sl].ap, nrmC.ap, brepC.ap, OP.add)
        nrmC.w, nrmC.r = e, []
        xfC.r.append(e); arepC.r.append(e); brepC.r.append(e)
        xn3[sl].w = e

    outc = Buf(sbt("outc", [128, 512], F32, stC3))
    xre2 = Buf(sbt("xre2", [128, 512], F32, stC3))

    def evac_ffn2(oc, row, off, w, bk):
        s.wait(SY, *xre2.r)
        xre2.w = ld(xre2.ap, xmid_h.rearrange("(a p) t -> a p t", p=128)[oc, :, off:off + w], q="dX")
        xre2.r = []
        s.wait(V, bk.w, xre2.w, *outc.r)
        e = Vw.scalar_tensor_tensor(outc.ap, bk.ap[0:128, 0:512], 0.25, xre2.ap,
                                    OP.mult, OP.add)
        bk.r.append(e); xre2.r.append(e)
        outc.w, outc.r = e, []
        s.wait(SY, outc.w)
        ed = s.dma(SY, "dO", out_h.rearrange("(a p) t -> a p t", p=128)[oc, :, off:off + w],
                   outc.ap)
        outc.r.append(ed)

    emit_ffn(xn3, "f2", evac_ffn2, stC3, CH_C, TR_)
    for q in ("dS", "dG", "dO", "dW", "dX"):
        for i in range(4):
            s.wait(SY, s.cur(f"{q}{i}"))
            s.wait(G, s.cur(f"{q}{i}"))
    s.wait(SY, s.cur("dC0"))
    s.wait(G, s.cur("dC0"))
    stC3.close()
    st.close()
    return nc


def kernel(**inp):
    x = np.asarray(inp["L0_attn_x"], dtype=np.float32)
    f32 = np.float32

    def c(a):
        return np.ascontiguousarray(np.asarray(a), dtype=f32)

    W = {
        "w_ip": c(np.asarray(inp["L0_attn_in_proj_w"]).T),
        "w_xp": c(np.asarray(inp["L0_attn_x_proj_w"]).T.reshape(NSL, 128, 96).transpose(1, 0, 2)),
        "w_op": c(np.asarray(inp["L0_attn_out_proj_w"]).T),
        "w_gl": c(np.asarray(inp["L0_attn_glu_w"]).T),
        "w_gate": c(np.asarray(inp["L0_ffn_w_gate"]).reshape(8, 128, NE).transpose(1, 0, 2)),
        "w_wd": c(np.asarray(inp["L0_ffn_Wd"]).transpose(1, 0, 2).reshape(D, EB)
                  .reshape(8, 128, EB).transpose(1, 0, 2)),
        "w_wu": c(np.asarray(inp["L0_ffn_Wu"]).reshape(EB, D)
                  .reshape(4, 128, D).transpose(1, 0, 2)),
        "bd": c(np.asarray(inp["L0_ffn_bd"]).reshape(EB).reshape(4, 128).T),
        "bu": c(inp["L0_ffn_bu"]),
        "ln_g": c(np.asarray(inp["L0_attn_ln_g"]).reshape(8, 128).T),
        "ln_b": c(np.asarray(inp["L0_attn_ln_b"]).reshape(8, 128).T),
        "conv_w": c(np.asarray(inp["L0_attn_conv_w"]).reshape(NSL, 128, 4).transpose(1, 2, 0)),
        "conv_b": c(np.asarray(inp["L0_attn_conv_b"]).reshape(NSL, 128).T),
        "glu_b": c(np.asarray(inp["L0_attn_glu_b"]).reshape(16, 128).T),
        "ident": np.eye(128, dtype=f32),
    }
    sel1 = np.zeros((8, 8, 128), dtype=f32)
    for cc_ in range(8):
        sel1[cc_, cc_, :] = 1.0
    W["sel1"] = np.ascontiguousarray(sel1.reshape(8, 8 * 128))
    sel8 = np.zeros((128, 8, 8), dtype=f32)
    for cc_ in range(8):
        sel8[:, cc_, cc_] = 1.0
    W["sel8"] = np.ascontiguousarray(sel8.reshape(128, 64))
    W["rep64"] = np.ascontiguousarray(np.repeat(np.eye(NE, dtype=f32), EB // NE, axis=1))

    w_dt_full = np.asarray(inp["L0_attn_dt_proj_w"]).T
    dtb_full = np.asarray(inp["L0_attn_dt_proj_b"]).reshape(NSL, 128)
    alog_full = np.asarray(inp["L0_attn_A_log"]).reshape(NSL, 128, NS)
    dv_full = np.asarray(inp["L0_attn_D"]).reshape(NSL, 128)

    nc = build_program()
    in_maps = []
    for core in range(8):
        b, h = core // 2, core % 2
        m = dict(W)
        xl = np.zeros((TL, D), dtype=f32)
        xl[R0:, :] = x[b, h * TR_:(h + 1) * TR_, :]
        if h == 1:
            xl[R0 - 3:R0, :] = x[b, h * TR_ - 3:h * TR_, :]
        m["x"] = c(xl.T)
        sls = slice(h * MYSL, (h + 1) * MYSL)
        m["w_dt"] = c(w_dt_full[:, h * MYSL * 128:(h + 1) * MYSL * 128])
        m["dt_b"] = c(dtb_full[sls].T)
        m["A_log"] = c(alog_full[sls].transpose(1, 0, 2).reshape(128, MYSL * NS))
        m["Dv"] = c(dv_full[sls].T)
        m["hmask"] = np.full((128, 1), float(h), dtype=f32)
        in_maps.append(m)
    res = run_bass_kernel_spmd(nc, in_maps, core_ids=list(range(8)))
    out = np.empty((4, 4096, 1024), dtype=np.float32)
    for core in range(8):
        b, h = core // 2, core % 2
        o = res.results[core]["out"]
        out[b, h * TR_:(h + 1) * TR_, :] = o.T
    return out
